# version 06 (final)
# speedup vs baseline: 1.3999x; 1.0141x over previous
import numpy as np
import ml_dtypes

import concourse.bass as bass
import concourse.mybir as mybir
from concourse.bass_utils import run_bass_kernel_spmd

F32 = mybir.dt.float32
BF16 = mybir.dt.bfloat16
AF = mybir.ActivationFunctionType
ALU = mybir.AluOpType
AX = mybir.AxisListType
ts = bass.ts

B = 2
S = 8192
D = 1024
T = 2048
NT = T // 128
NTS = S // 128
NB = T // 512
NBS = S // 512
HEADS = 16
HD = 64
DFF = 4096
EPS = 1e-5

L1_HID = 2048
MOE = 64
NEXP = 8
EXPH = 1024



def _rope_tables():
    inv_freq = 1.0 / 10000.0 ** (np.arange(0, HD, 2, dtype=np.float64) / HD)
    ang = np.arange(S, dtype=np.float64)[None, :] * inv_freq[:, None]
    return np.cos(ang), np.sin(ang)


def _perm_grp4():
    perm = []
    for m in range(4):
        for h in range(4 * m, 4 * m + 4):
            perm.extend(h * HD + np.arange(0, HD, 2))
        for h in range(4 * m, 4 * m + 4):
            perm.extend(h * HD + np.arange(1, HD, 2))
    return np.array(perm)


def _row(v):
    return np.ascontiguousarray(np.asarray(v, np.float32).reshape(1, -1))


def _cols(v, p=128):
    return np.ascontiguousarray(np.asarray(v, np.float32).reshape(-1, p).T)


def make_in_maps(inp):
    cosS, sinS = _rope_tables()
    perm = _perm_grp4()

    x = np.asarray(inp["L0_attn_x"], np.float32)
    wq = np.ascontiguousarray(np.asarray(inp["L0_attn_Wq"], np.float32)[:, perm])
    wk = np.ascontiguousarray(np.asarray(inp["L0_attn_Wk"], np.float32)[:, perm])
    bq = np.asarray(inp["L0_attn_bq"], np.float32)[perm]
    bk = np.asarray(inp["L0_attn_bk"], np.float32)[perm]

    common = {
        "wq": wq, "wk": wk,
        "wv": np.asarray(inp["L0_attn_Wv"], np.float32),
        "wo": np.asarray(inp["L0_attn_Wo"], np.float32),
        "w1": np.asarray(inp["L0_attn_W1"], np.float32),
        "w2": np.asarray(inp["L0_attn_W2"], np.float32),
        "bq_r": _row(bq), "bk_r": _row(bk),
        "bv_r": _row(inp["L0_attn_bv"]),
        "bo_r": _row(inp["L0_attn_bo"]),
        "b1_r": _row(inp["L0_attn_b1"]),
        "b2_r": _row(inp["L0_attn_b2"]),
        "b1_c": np.ascontiguousarray(np.asarray(inp["L0_attn_b1"], np.float32).reshape(32, 128).T),
        "b2_c": np.ascontiguousarray(np.asarray(inp["L0_attn_b2"], np.float32).reshape(8, 128).T),
        "e_b1_c": np.ascontiguousarray(np.asarray(inp["L1_ffn_e_b1"], np.float32).reshape(8, 8, 128).transpose(0, 2, 1)),
        "e_b2_c": np.ascontiguousarray(np.asarray(inp["L1_ffn_e_b2"], np.float32).reshape(8, 64, 1)),
        "g1_c": _cols(inp["L0_attn_g1"]), "be1_c": _cols(inp["L0_attn_beta1"]),
        "g2_c": _cols(inp["L0_attn_g2"]), "be2_c": _cols(inp["L0_attn_beta2"]),
        "router_w": np.asarray(inp["L0_ffn_router_w"], np.float32),
        "router_b": _row(inp["L0_ffn_router_b"]),
        "expert_w": np.asarray(inp["L0_ffn_expert_w"], np.float32),
        "expert_b": np.asarray(inp["L0_ffn_expert_b"], np.float32),
        "ident": np.eye(128, dtype=ml_dtypes.bfloat16),
        "l1_wqkvT": np.ascontiguousarray(np.asarray(inp["L1_attn_w_qkv"], np.float32).T),
        "l1_woutT": np.ascontiguousarray(np.asarray(inp["L1_attn_w_out"], np.float32).T),
        "l1_bout_r": _row(inp["L1_attn_b_out"]),
        "l1_proj": np.asarray(inp["L1_attn_proj"], np.float32),
        "enc_w": np.asarray(inp["L1_ffn_enc_w"], np.float32),
        "enc_b_r": _row(inp["L1_ffn_enc_b"]),
        "enc_b_cc": np.ascontiguousarray(np.asarray(inp["L1_ffn_enc_b"], np.float32).reshape(16, 128).T),
        "c2s_b_cc": np.ascontiguousarray(np.asarray(inp["L1_ffn_c2s_b"], np.float32).reshape(16, 128).T),
        "s2c_w": np.asarray(inp["L1_ffn_s2c_w"], np.float32),
        "s2c_b_r": _row(inp["L1_ffn_s2c_b"]),
        "r_w1": np.asarray(inp["L1_ffn_r_w1"], np.float32),
        "r_b1_r": _row(inp["L1_ffn_r_b1"]),
        "r_w2": np.asarray(inp["L1_ffn_r_w2"], np.float32),
        "r_b2_r": _row(inp["L1_ffn_r_b2"]),
        "e_w1": np.asarray(inp["L1_ffn_e_w1"], np.float32),
        "e_b1": np.asarray(inp["L1_ffn_e_b1"], np.float32),
        "e_w2": np.asarray(inp["L1_ffn_e_w2"], np.float32),
        "e_b2": np.asarray(inp["L1_ffn_e_b2"], np.float32),
        "c2s_w": np.asarray(inp["L1_ffn_c2s_w"], np.float32),
        "c2s_b_r": _row(inp["L1_ffn_c2s_b"]),
        "dec_w": np.asarray(inp["L1_ffn_dec_w"], np.float32),
        "dec_b_r": _row(inp["L1_ffn_dec_b"]),
        "ln_g_bc": np.ascontiguousarray(np.broadcast_to(
            np.asarray(inp["L1_ffn_ln_g"], np.float32)[None, :], (128, D))),
        "ln_b_bc": np.ascontiguousarray(np.broadcast_to(
            np.asarray(inp["L1_ffn_ln_b"], np.float32)[None, :], (128, D))),
    }

    dup = np.tile(np.arange(32), 4)
    in_maps = []
    for cc in range(8):
        b, r = divmod(cc, 4)
        order = np.r_[np.arange(r * T, (r + 1) * T),
                      np.arange(0, r * T), np.arange((r + 1) * T, S)]
        m = dict(common)
        m["x_full"] = np.ascontiguousarray(x[b][order])
        m["rqc"] = np.ascontiguousarray(cosS[dup][:, order[:T]] / 8.0).astype(np.float32)
        m["rqs"] = np.ascontiguousarray(sinS[dup][:, order[:T]] / 8.0).astype(np.float32)
        m["rkc"] = np.ascontiguousarray(cosS[dup][:, order]).astype(np.float32)
        m["rks"] = np.ascontiguousarray(sinS[dup][:, order]).astype(np.float32)
        in_maps.append(m)
    return in_maps



import os as _os
P7B_STOP = int(_os.environ.get("P7B_STOP", "4"))


class Ctx:
    def __init__(self, debug_taps=()):
        self.nc = bass.Bass("TRN2", target_bir_lowering=False)
        self.taps = set(debug_taps)
        self.dbg = {}
        self._es = self.nc.ctx

    def sb(self, name, shape, dtype):
        return self._es.enter_context(self.nc.sbuf_tensor(name, shape, dtype))

    def sem(self, name):
        return self._es.enter_context(self.nc.semaphore(name))

    def tap(self, name, shape, dtype=F32):
        if name in self.taps:
            self.dbg[name] = self.nc.dram_tensor(name, shape, dtype, kind="ExternalOutput").ap()
            return self.dbg[name]
        return None


def mk_sem_scope(nc, es):
    handles = []

    def _cleanup():
        if handles:
            nc.clear_and_free_semaphores(handles)
            nc.all_engine_barrier()

    es.callback(_cleanup)

    def _sem(n):
        h = nc.alloc_semaphore(name=n)
        handles.append(h)
        return h

    return _sem


def build_program(debug_taps=(), upto=99):
    c = Ctx(debug_taps)
    nc = c.nc

    def din(name, shape, dtype=F32):
        return nc.dram_tensor(name, shape, dtype, kind="ExternalInput").ap()

    x_full = din("x_full", [S, D])
    wq_h = din("wq", [D, D]); wk_h = din("wk", [D, D]); wv_h = din("wv", [D, D])
    wo_h = din("wo", [D, D]); w1_h = din("w1", [D, DFF]); w2_h = din("w2", [DFF, D])
    bq_r = din("bq_r", [1, D]); bk_r = din("bk_r", [1, D])
    bv_r = din("bv_r", [1, D]); bo_r = din("bo_r", [1, D])
    b1_r = din("b1_r", [1, DFF]); b2_r = din("b2_r", [1, D])
    b1_c_h = din("b1_c", [128, 32]); b2_c_h = din("b2_c", [128, 8])
    e_b1_c_h = din("e_b1_c", [8, 128, 8]); e_b2_c_h = din("e_b2_c", [8, 64, 1])
    g1_c = din("g1_c", [128, 8]); be1_c = din("be1_c", [128, 8])
    g2_c = din("g2_c", [128, 8]); be2_c = din("be2_c", [128, 8])
    router_w_h = din("router_w", [D, 8]); router_b_h = din("router_b", [1, 8])
    expert_w_h = din("expert_w", [8, D, D]); expert_b_h = din("expert_b", [8, D])
    ident_h = din("ident", [128, 128], BF16)
    rqc_h = din("rqc", [128, T]); rqs_h = din("rqs", [128, T])
    rkc_h = din("rkc", [128, S]); rks_h = din("rks", [128, S])
    l1_wqkvT_h = din("l1_wqkvT", [D, 3 * D]); l1_woutT_h = din("l1_woutT", [D, D])
    l1_bout_r = din("l1_bout_r", [1, D]); l1_proj_h = din("l1_proj", [8, 128, 128])
    enc_w_h = din("enc_w", [D, L1_HID]); enc_b_r = din("enc_b_r", [1, L1_HID])
    s2c_w_h = din("s2c_w", [L1_HID, MOE]); s2c_b_r = din("s2c_b_r", [1, MOE])
    r_w1_h = din("r_w1", [64, 64]); r_b1_r = din("r_b1_r", [1, 64])
    r_w2_h = din("r_w2", [64, 8]); r_b2_r = din("r_b2_r", [1, 8])
    e_w1_h = din("e_w1", [8, 64, EXPH]); e_b1_h = din("e_b1", [8, EXPH])
    e_w2_h = din("e_w2", [8, EXPH, 64]); e_b2_h = din("e_b2", [8, 64])
    c2s_w_h = din("c2s_w", [MOE, L1_HID]); c2s_b_r = din("c2s_b_r", [1, L1_HID])
    enc_bc_h = din("enc_b_cc", [128, 16]); c2s_bc_h = din("c2s_b_cc", [128, 16])
    dec_w_h = din("dec_w", [L1_HID, D]); dec_b_r = din("dec_b_r", [1, D])
    ln_g_bc_h = din("ln_g_bc", [128, D]); ln_b_bc_h = din("ln_b_bc", [128, D])

    out = nc.dram_tensor("out_own", [T, D], F32, kind="ExternalOutput").ap()

    k_sc = nc.dram_tensor("k_sc", [HEADS, 64, S], BF16, kind="Internal").ap()
    v_sc = nc.dram_tensor("v_sc", [HEADS, S, 65], BF16, kind="Internal").ap()
    x1_sc = nc.dram_tensor("x1_sc", [T, D], F32, kind="Internal").ap()
    xn1_sc = nc.dram_tensor("xn1_sc", [128, 8 * T], BF16, kind="Internal").ap()
    qk1_sc = nc.dram_tensor("qk1_sc", [16, 128, T], BF16, kind="Internal").ap()
    v1_sc2 = nc.dram_tensor("v1_sc2", [NT * 4, 32, 1024], BF16, kind="Internal").ap()
    o1_sc = nc.dram_tensor("o1_sc", [8, 128, T], BF16, kind="Internal").ap()

    from contextlib import ExitStack
    ident = c.sb("ident_s", [128, 128], BF16)
    ones_r = c.sb("ones_r", [1, 512], F32)
    ln0 = c.sb("ln0", [128, 2 * NT], F32)
    ln1 = c.sb("ln1", [128, 2 * NT], F32)
    bqp = c.sb("bqp", [128, 8], F32)
    bkp = c.sb("bkp", [128, 8], F32)

    arena = c.sb("arena", [128, 32768], F32)
    AR = nc.lookup_mloc(arena).addr

    def at(name, shape, dtype, off):
        return nc.alloc_sbuf_tensor_at(name, shape, dtype, offset=AR + off)

    OFF_A, OFF_B, OFF_C, OFF_D = 0, 32768, 65536, 98304
    xn_fm = at("xn_fm", [128, 8 * T], BF16, OFF_A)
    x2fm = at("x2fm", [128, 8 * T], BF16, OFF_A)
    q_pair = [at(f"q_pair{i}", [128, T], BF16, OFF_B + 4096 * i) for i in range(8)]
    hblk = at("hblk", [128, 32 * 512], BF16, OFF_B)
    xn1_fm = at("xn1_fm", [128, 8 * T], BF16, OFF_B)
    wq16 = at("wq16", [128, 8 * D], BF16, OFF_C)
    wk16 = at("wk16", [128, 8 * D], BF16, OFF_C + 16384)
    g_tm = at("g_tm", [128, NT * D], BF16, OFF_C)
    wv16 = at("wv16", [128, 8 * D], BF16, OFF_D)
    bvp = at("bvp", [1, D], F32, OFF_D + 16384)
    kst = at("kst", [128, 2 * 1024], BF16, OFF_D + 20480)
    vst = at("vst", [128, 2 * (16 * 65)], BF16, OFF_D + 24576)
    kaug = [at("kaug0", [65, S], BF16, OFF_D), at("kaug1", [65, S], BF16, OFF_D + 16384)]
    h1_tm = at("h1_tm", [128, NT * D], BF16, OFF_D)

    with ExitStack() as es:
        es.enter_context(nc.named_scope("p0a"))
        _sem = mk_sem_scope(nc, es)
        ld = _sem("p0a_ld")
        blk = es.enter_context(nc.Block())
        small = [(ident, ident_h)]

        @blk.sync
        def _(sync):
            for dst, src in small:
                sync.dma_start(out=dst[:, :], in_=src).then_inc(ld, 16)

        @blk.vector
        def _(vector):
            vector.memset(ones_r[:, :], 1.0)
            vector.wait_ge(ld, 16 * len(small))

    if upto <= 0:
        return c
    with ExitStack() as es:
        es.enter_context(nc.named_scope("p0b"))
        _sem = mk_sem_scope(nc, es)
        wstage = es.enter_context(nc.sbuf_tensor("wstage", [128, 2 * D], F32))
        smallv = es.enter_context(nc.sbuf_tensor("smallv", [128, 16], F32))
        brows = es.enter_context(nc.sbuf_tensor("brows", [1, 3 * D], F32))
        psb_q = es.enter_context(nc.psum_tensor("psb_q", [1, D], F32))
        psb_k = es.enter_context(nc.psum_tensor("psb_k", [1, D], F32))
        psb_v = es.enter_context(nc.psum_tensor("psb_v", [1, D], F32))
        ld_s = _sem("p0b_lds"); ld_w0 = _sem("p0b_ldw0"); ld_w1 = _sem("p0b_ldw1")
        ld_b = _sem("p0b_ldb"); done_v = _sem("p0b_done_v")
        done_t = _sem("p0b_done_t"); fin = _sem("p0b_fin")
        blk = es.enter_context(nc.Block())
        g1s = smallv[:, 0:8]; be1s = smallv[:, 8:16]
        NW = 24

        @blk.sync
        def _(sync):
            for dst, srch in [(g1s, g1_c), (be1s, be1_c)]:
                sync.dma_start(out=dst, in_=srch).then_inc(ld_s, 16)
            for i, srch in enumerate([bq_r, bk_r, bv_r]):
                sync.dma_start(out=brows[:, ts(i, D)], in_=srch).then_inc(ld_s, 16)
            for k in range(NW):
                wi, i = divmod(k, 8)
                wsrc = [wq_h, wk_h, wv_h][wi]
                if k >= 2:
                    sync.wait_ge(done_v, k - 1)
                    sync.wait_ge(done_t, k - 1)
                sync.dma_start(out=wstage[:, ts(k % 2, D)],
                               in_=wsrc[ts(i, 128), :]).then_inc(ld_w1 if k % 2 else ld_w0, 16)
            sync.wait_ge(fin, 1)
            for j in range(8):
                sync.dma_start(out=bqp[:, j:j + 1], in_=brows[0:1, 128 * j:128 * (j + 1)],
                               allow_slow_non_contiguous=True).then_inc(ld_b, 16)
                sync.dma_start(out=bkp[:, j:j + 1], in_=brows[0:1, D + 128 * j:D + 128 * (j + 1)],
                               allow_slow_non_contiguous=True).then_inc(ld_b, 16)
            sync.wait_ge(ld_b, 16 * 16)

        @blk.vector
        def _(vector):
            vector.wait_ge(ld_s, 16 * 5)
            for k in range(NW):
                wi, i = divmod(k, 8)
                w16 = [wq16, wk16, wv16][wi]
                vector.wait_ge(ld_w1 if k % 2 else ld_w0, 16 * (k // 2 + 1))
                vector.tensor_scalar(w16[:, ts(i, D)], wstage[:, ts(k % 2, D)],
                                     g1s[:, i:i + 1], None, ALU.mult).then_inc(done_v, 1)
            vector.wait_ge(done_t, 8)
            vector.tensor_tensor(brows[:, 0:D], brows[:, 0:D], psb_q[:, :], ALU.add)
            vector.wait_ge(done_t, 16)
            vector.tensor_tensor(brows[:, D:2 * D], brows[:, D:2 * D], psb_k[:, :], ALU.add)
            vector.wait_ge(done_t, 24)
            vector.tensor_tensor(bvp[:, :], brows[:, 2 * D:3 * D], psb_v[:, :],
                                 ALU.add).then_inc(fin, 1)

        @blk.tensor
        def _(tensor):
            tensor.wait_ge(ld_s, 16 * 5)
            for k in range(NW):
                wi, i = divmod(k, 8)
                tgt = [psb_q, psb_k, psb_v][wi]
                tensor.wait_ge(ld_w1 if k % 2 else ld_w0, 16 * (k // 2 + 1))
                for j in range(2):
                    mm = tensor.matmul(tgt[:, ts(j, 512)],
                                       be1s[:, i:i + 1],
                                       wstage[:, (k % 2) * D + j * 512:(k % 2) * D + (j + 1) * 512],
                                       start=(i == 0), stop=(i == 7))
                mm.then_inc(done_t, 1)

    if upto <= 1:
        return c
    pair_sched = []
    for b in range(NBS):
        for m in range(4):
            pair_sched.append((b, "k", m))
        if b < NB:
            for m in range(4):
                pair_sched.append((b, "q", m))
    NPAIR = len(pair_sched)
    pair_of = {}
    for idx, (b, kind, m) in enumerate(pair_sched):
        pair_of[(b, kind, m)] = idx
    pair_ndma = [4 if kind == "k" else 8 for (_, kind, _) in pair_sched]
    pair_cum_par = []
    tot = [0, 0]
    for i, nd in enumerate(pair_ndma):
        tot[i % 2] += nd
        pair_cum_par.append(tot[i % 2])

    from contextlib import ExitStack
    with ExitStack() as es:
        es.enter_context(nc.named_scope("p1"))
        def _sb(n, s, d):
            return es.enter_context(nc.sbuf_tensor(n, s, d))

        def _ps(n, s, d=F32):
            return es.enter_context(nc.psum_tensor(n, s, d))

        _sem = mk_sem_scope(nc, es)

        xbuf = _sb("xbuf", [128, 2 * D], F32)
        xc32 = _sb("xc32", [128, 2 * D], F32)
        xn32 = _sb("xn32", [128, 2 * D], F32)
        xc2 = xc32
        xh16 = _sb("xh16", [128, 2 * D], BF16)
        xn16 = _sb("xn16", [128, 2 * D], BF16)
        stat = _sb("stat", [128, 2 * 16], F32)
        xhfm = _sb("xhfm", [128, 2 * 4096], BF16)
        ropet = _sb("ropet", [128, 2 * 512], F32)
        tbk = _sb("tbk", [128, 2 * 1024], F32)
        tbq = _sb("tbq", [128, 2 * 1024], F32)
        ps_tph = _ps("ps_tph", [128, 1024], BF16)
        ps_tpn = _ps("ps_tpn", [128, 1024], BF16)
        ps_qk = _ps("ps_qk", [128, 1024], F32)
        ps_v = _ps("ps_v", [128, 1024], F32)
        ldx2 = [_sem("p1_ldx0"), _sem("p1_ldx1")]
        ldtk2 = [_sem("p1_ldtk0"), _sem("p1_ldtk1")]
        ldtq2 = [_sem("p1_ldtq0"), _sem("p1_ldtq1")]
        xfree = _sem("p1_xfree")
        v2s = _sem("p1_v2s"); s2v = _sem("p1_s2v")
        t2v = _sem("p1_t2v"); t2vn = _sem("p1_t2vn"); vfm = _sem("p1_vfm")
        evn = _sem("p1_evn"); mmk = _sem("p1_mmk")
        mmv = _sem("p1_mmv"); evk = _sem("p1_evk"); evv = _sem("p1_evv")
        stk2 = [_sem("p1_stk0"), _sem("p1_stk1")]
        stv2 = [_sem("p1_stv0"), _sem("p1_stv1")]
        blk = es.enter_context(nc.Block())
        @blk.sync
        def _(sync):
            nld = [0]

            def tbl_loads_k(b):
                if b >= 2:
                    sync.wait_ge(evk, pair_of[(b - 2, "k", 3)] + 1)
                sync.dma_start(out=tbk[:, (b % 2) * 1024:(b % 2) * 1024 + 512],
                               in_=rkc_h[:, 512 * b:512 * (b + 1)]).then_inc(ldtk2[b % 2], 16)
                sync.dma_start(out=tbk[:, (b % 2) * 1024 + 512:(b % 2) * 1024 + 1024],
                               in_=rks_h[:, 512 * b:512 * (b + 1)]).then_inc(ldtk2[b % 2], 16)

            def tbl_loads_q(b):
                if b >= 2:
                    sync.wait_ge(evk, pair_of[(b - 2, "q", 3)] + 1)
                sync.dma_start(out=tbq[:, (b % 2) * 1024:(b % 2) * 1024 + 512],
                               in_=rqc_h[:, 512 * b:512 * (b + 1)]).then_inc(ldtq2[b % 2], 16)
                sync.dma_start(out=tbq[:, (b % 2) * 1024 + 512:(b % 2) * 1024 + 1024],
                               in_=rqs_h[:, 512 * b:512 * (b + 1)]).then_inc(ldtq2[b % 2], 16)

            tbl_loads_q(0); tbl_loads_q(1)
            tbl_loads_k(0); tbl_loads_k(1)
            for g in range(NTS):
                if g >= 2:
                    sync.wait_ge(xfree, g - 1)
                sync.dma_start(out=xbuf[:, ts(g % 2, D)],
                               in_=x_full[ts(g, 128), :]).then_inc(ldx2[g % 2], 16)
                if g % 4 == 0 and 2 <= g // 4 < NBS:
                    tbl_loads_k(g // 4)
                    if g // 4 < NB:
                        tbl_loads_q(g // 4)


        @blk.vector
        def _(vector):
            def ln_chain(g, inner):
                par = g % 2
                sl = stat[:, 16 * par:16 * (par + 1)]
                src = xbuf[:, ts(par, D)] if not inner else xn32[:, ts(par, D)]
                dst = xc32[:, ts(par, D)] if not inner else xc2[:, ts(par, D)]
                vector.bn_stats(sl[:, 0:6], src[:, 0:512])
                vector.bn_stats(sl[:, 6:12], src[:, 512:1024])
                vector.drain()
                vector.bn_aggr(sl[:, 12:14], sl[:, 0:12])
                vector.drain()
                sub = vector.tensor_scalar(dst, src, sl[:, 12:13], None, ALU.subtract)
                if not inner:
                    sub.then_inc(xfree, 1)
                vector.tensor_scalar(sl[:, 14:15], sl[:, 13:14], EPS, None, ALU.add)
                vector.drain()
                if not inner and g < NT:
                    vector.tensor_copy(ln0[:, 2 * g:2 * g + 1], sl[:, 12:13])
                vector.reciprocal(sl[:, 15:16], sl[:, 14:15]).then_inc(v2s, 1)

            def evac_tile(g):
                b, t = divmod(g, 4)
                bpar = b % 2
                vector.wait_ge(t2v, g + 1)
                if b >= 2:
                    vector.wait_ge(mmv, 4 * (b - 1))
                vector.tensor_copy(
                    xhfm[:, 4096 * bpar:4096 * (bpar + 1)]
                        .rearrange("p (j c) -> p j c", j=8)[:, :, 128 * t:128 * (t + 1)],
                    ps_tph[:, :].rearrange("p (j c) -> p j c", j=8),
                ).then_inc(vfm, 1)
                if g < NT:
                    vector.wait_ge(t2vn, g + 1)
                    vector.tensor_copy(
                        xn_fm[:, :].rearrange("p (j c) -> p j c", j=8)[:, :, 128 * g:128 * (g + 1)],
                        ps_tpn[:, :].rearrange("p (j c) -> p j c", j=8),
                    ).then_inc(evn, 1)

            def rope_pair(b, kind, m):
                pi = pair_of[(b, kind, m)]
                vector.wait_ge(mmk, pi + 1)
                if pi >= 2:
                    vector.wait_ge(stk2[pi % 2], 16 * pair_cum_par[pi - 2])
                tb = tbk if kind == "k" else tbq
                bias = bkp if kind == "k" else bqp
                if kind == "k":
                    vector.wait_ge(ldtk2[b % 2], 32 * (b // 2 + 1))
                else:
                    vector.wait_ge(ldtq2[b % 2], 32 * (b // 2 + 1))
                bslot = b % 2
                cosb = tb[:, bslot * 1024:bslot * 1024 + 512]
                sinb = tb[:, bslot * 1024 + 512:bslot * 1024 + 1024]
                pe = ps_qk[:, 0:512]
                po = ps_qk[:, 512:1024]
                be = bias[:, 2 * m:2 * m + 1]
                bo = bias[:, 2 * m + 1:2 * m + 2]
                t1 = ropet[:, 0:512]; t2 = ropet[:, 512:1024]
                dst = kst[:, (pi % 2) * 1024:(pi % 2 + 1) * 1024]
                vector.scalar_tensor_tensor(t1, pe, be, cosb, ALU.add, ALU.mult)
                vector.scalar_tensor_tensor(t2, po, bo, sinb, ALU.add, ALU.mult)
                vector.drain()
                vector.tensor_tensor(dst[:, 0:512], t1, t2, ALU.subtract)
                vector.drain()
                vector.scalar_tensor_tensor(t1, pe, be, sinb, ALU.add, ALU.mult)
                vector.scalar_tensor_tensor(t2, po, bo, cosb, ALU.add, ALU.mult)
                vector.drain()
                vector.tensor_tensor(dst[:, 512:1024], t1, t2, ALU.add).then_inc(evk, 1)

            def v_evac(g):
                vector.wait_ge(mmv, g + 1)
                if g >= 2:
                    vector.wait_ge(stv2[g % 2], 16 * (g // 2))
                vector.tensor_copy(
                    vst[:, (16 * 65) * (g % 2):(16 * 65) * (g % 2 + 1)]
                       .rearrange("p (h e) -> p h e", h=16)[:, :, 0:64],
                    ps_v[:, :].rearrange("p (h e) -> p h e", h=16),
                ).then_inc(evv, 1)

            vector.memset(vst[:, :].rearrange("p (h e) -> p h e", h=32)[:, :, 64:65], 1.0)
            vector.drain()

            for g in range(NTS):
                b, t = divmod(g, 4)
                if g >= 2:
                    vector.wait_ge(s2v, 2 * (g - 1))
                vector.wait_ge(ldx2[g % 2], 16 * (g // 2 + 1))
                ln_chain(g, inner=False)
                vector.wait_ge(s2v, 2 * g + 1)
                ln_chain(g, inner=True)
                if g >= 1:
                    evac_tile(g - 1)
                if t == 0 and b >= 1:
                    bb = b - 1
                    for m in range(4):
                        rope_pair(bb, "k", m)
                    if bb < NB:
                        for m in range(4):
                            rope_pair(bb, "q", m)
                    for tt in range(4):
                        v_evac(4 * bb + tt)
            evac_tile(NTS - 1)
            bb = NBS - 1
            for m in range(4):
                rope_pair(bb, "k", m)
            for tt in range(4):
                v_evac(4 * bb + tt)

        @blk.scalar
        def _(scalar):
            for g in range(NTS):
                par = g % 2
                sl = stat[:, 16 * par:16 * (par + 1)]
                scalar.wait_ge(v2s, 2 * g + 1)
                scalar.activation(sl[:, 14:15], sl[:, 15:16], AF.Sqrt)
                scalar.drain()
                if g < NT:
                    scalar.activation(ln0[:, 2 * g + 1:2 * g + 2], sl[:, 15:16], AF.Sqrt)
                if g >= 2:
                    scalar.wait_ge(t2vn if g < NT else t2v, g - 1)
                ac = scalar.activation(xn32[:, ts(par, D)], xc32[:, ts(par, D)],
                                       AF.Identity, bias=0.0, scale=sl[:, 14:15])
                if g < NT:
                    scalar.drain()
                    ac = scalar.activation(xn16[:, ts(par, D)], xn32[:, ts(par, D)],
                                           AF.Identity, bias=0.0, scale=1.0)
                ac.then_inc(s2v, 1)
                scalar.wait_ge(v2s, 2 * g + 2)
                scalar.activation(sl[:, 14:15], sl[:, 15:16], AF.Sqrt)
                scalar.drain()
                if g >= 2:
                    scalar.wait_ge(t2v, g - 1)
                scalar.activation(xh16[:, ts(par, D)], xc2[:, ts(par, D)],
                                  AF.Identity, bias=0.0,
                                  scale=sl[:, 14:15]).then_inc(s2v, 1)

        @blk.tensor
        def _(tensor):
            def block_tail(b):
                bpar = b % 2
                tensor.wait_ge(vfm, 4 * (b + 1))
                kinds = [("k", m) for m in range(4)]
                if b < NB:
                    kinds += [("q", m) for m in range(4)]
                for kind, m in kinds:
                    pi = pair_of[(b, kind, m)]
                    if pi >= 1:
                        tensor.wait_ge(evk, pi)
                    w16 = wk16 if kind == "k" else wq16
                    for half in range(2):
                        oc = 2 * m + half
                        for ic in range(8):
                            mm = tensor.matmul(ps_qk[:, ts(half, 512)],
                                               w16[:, D * ic + 128 * oc:D * ic + 128 * (oc + 1)],
                                               xhfm[:, 4096 * bpar + 512 * ic:4096 * bpar + 512 * (ic + 1)],
                                               start=(ic == 0), stop=(ic == 7))
                    mm.then_inc(mmk, 1)
                for t in range(4):
                    g = 4 * b + t
                    if g >= 1:
                        tensor.wait_ge(evv, g)
                    for half in range(2):
                        seg = ps_v[:, ts(half, 512)]
                        tensor.matmul(seg, ones_r[0:1, 0:128], bvp[:, ts(half, 512)],
                                      start=True, stop=False)
                        for ic in range(8):
                            mm = tensor.matmul(seg,
                                               xhfm[:, 4096 * bpar + 512 * ic + 128 * t:4096 * bpar + 512 * ic + 128 * (t + 1)],
                                               wv16[:, D * ic + 512 * half:D * ic + 512 * (half + 1)],
                                               start=False, stop=(ic == 7))
                    mm.then_inc(mmv, 1)

            for g in range(NTS):
                b, t = divmod(g, 4)
                tensor.wait_ge(s2v, 2 * g + 2)
                if g >= 1:
                    tensor.wait_ge(vfm, g)
                for j in range(8):
                    tp = tensor.transpose(ps_tph[:, 128 * j:128 * (j + 1)],
                                          xh16[:, D * (g % 2) + 128 * j:D * (g % 2) + 128 * (j + 1)],
                                          ident[:, :])
                tp.then_inc(t2v, 1)
                if g < NT:
                    if g >= 1:
                        tensor.wait_ge(evn, g)
                    for j in range(8):
                        tp = tensor.transpose(ps_tpn[:, 128 * j:128 * (j + 1)],
                                              xn16[:, D * (g % 2) + 128 * j:D * (g % 2) + 128 * (j + 1)],
                                              ident[:, :])
                    tp.then_inc(t2vn, 1)
                if t == 3:
                    block_tail(b)

        @blk.gpsimd
        def _(gpsimd):
            for b in range(NBS):
                kinds = [("k", m) for m in range(4)]
                if b < NB:
                    kinds += [("q", m) for m in range(4)]
                for kind, m in kinds:
                    pi = pair_of[(b, kind, m)]
                    gpsimd.wait_ge(evk, pi + 1)
                    src = kst[:, (pi % 2) * 1024:(pi % 2 + 1) * 1024]
                    if kind == "k":
                        for hh in range(4):
                            h = 4 * m + hh
                            gpsimd.dma_start(
                                out=k_sc[h, :, 512 * b:512 * (b + 1)]
                                    .rearrange("(half p) e -> p half e", half=2),
                                in_=src[32 * hh:32 * hh + 32, :]
                                    .rearrange("p (half e) -> p half e", half=2),
                            ).then_inc(stk2[pi % 2], 16)
                    else:
                        for hh in range(4):
                            pp = 2 * m + hh // 2
                            ro = 64 * (hh % 2)
                            for half in range(2):
                                gpsimd.dma_start(
                                    out=q_pair[pp][ro + 32 * half:ro + 32 * (half + 1),
                                                   512 * b:512 * (b + 1)],
                                    in_=src[32 * hh:32 * hh + 32, 512 * half:512 * (half + 1)],
                                ).then_inc(stk2[pi % 2], 16)
                for t in range(4):
                    g = 4 * b + t
                    gpsimd.wait_ge(evv, g + 1)
                    src = vst[:, (16 * 65) * (g % 2):(16 * 65) * (g % 2 + 1)]
                    gpsimd.dma_start(
                        out=v_sc[:, 128 * g:128 * (g + 1), :].rearrange("h p e -> p h e"),
                        in_=src.rearrange("p (h e) -> p h e", h=16)).then_inc(stv2[g % 2], 16)
            gpsimd.wait_ge(stk2[0], 16 * pair_cum_par[NPAIR - 2])
            gpsimd.wait_ge(stk2[1], 16 * pair_cum_par[NPAIR - 1])
            gpsimd.wait_ge(stv2[0], 16 * (NTS // 2))
            gpsimd.wait_ge(stv2[1], 16 * (NTS // 2))

    if upto <= 2:
        return c

    with ExitStack() as es:
        es.enter_context(nc.named_scope("p2"))
        def _sb(n, s, d):
            return es.enter_context(nc.sbuf_tensor(n, s, d))

        def _ps(n, s, d=F32):
            return es.enter_context(nc.psum_tensor(n, s, d))

        _sem = mk_sem_scope(nc, es)

        rw32 = _sb("rw32", [128, 64], F32)
        rw_s = _sb("rw_s", [128, 64], BF16)
        rb_s = _sb("rb_s", [1, 8], F32)
        estage = _sb("estage", [128, 2 * D], F32)
        ew16 = [_sb("ew16_0", [128, 8 * D], BF16), _sb("ew16_1", [128, 8 * D], BF16)]
        eb_s = _sb("eb_s", [1, 2 * D], F32)
        p2st = _sb("p2st", [128, 2 * 24], F32)
        disp = _sb("disp", [128, NT * 8], F32)
        ps_r2 = [_ps("ps_r0", [128, 8]), _ps("ps_r1", [128, 8])]
        ps_e = _ps("ps_e", [128, 2 * D])
        ld_r = _sem("p2_ldr")
        ld_e = [_sem("p2_lde0"), _sem("p2_lde1")]
        ld_eb = [_sem("p2_ldeb0"), _sem("p2_ldeb1")]
        mmr = _sem("p2_mmr"); r2s = _sem("p2_r2s"); s2r = _sem("p2_s2r")
        ev_r = _sem("p2_evr"); cast_e = _sem("p2_cast")
        mme = _sem("p2_mme"); ev_e = _sem("p2_eve")
        blk = es.enter_context(nc.Block())

        @blk.sync
        def _(sync):
            sync.dma_start(out=rw32[:, :].rearrange("p (i e) -> p i e", i=8),
                           in_=router_w_h.rearrange("(i p) e -> p i e", p=128)).then_inc(ld_r, 16)
            sync.dma_start(out=rb_s[:, :], in_=router_b_h).then_inc(ld_r, 16)
            for e in range(NEXP):
                for i in range(8):
                    k = 8 * e + i
                    if e >= 2:
                        sync.wait_ge(mme, NT * (e - 1))
                    if k >= 2:
                        sync.wait_ge(cast_e, k)
                    sync.dma_start(out=estage[:, ts(k % 2, D)],
                                   in_=expert_w_h[e, ts(i, 128), :]).then_inc(ld_e[k % 2], 16)
                if e >= 2:
                    sync.wait_ge(mme, NT * (e - 1))
                sync.dma_start(out=eb_s[:, ts(e % 2, D)],
                               in_=expert_b_h[e:e + 1, :]).then_inc(ld_eb[e % 2], 16)

        eld_chunk = {(e, i): (8 * e + i) // 2 + 1 for e in range(NEXP) for i in range(8)}

        @blk.vector
        def _(vector):
            vector.wait_ge(ld_r, 32)
            vector.tensor_copy(rw_s[:, :], rw32[:, :]).then_inc(cast_e, 1)
            for e in range(NEXP):
                for i in range(8):
                    k = 8 * e + i
                    vector.wait_ge(ld_e[k % 2], 16 * eld_chunk[(e, i)])
                    if e >= 2:
                        vector.wait_ge(mme, NT * (e - 1))
                    vector.tensor_copy(ew16[e % 2][:, ts(i, D)],
                                       estage[:, ts(k % 2, D)]).then_inc(cast_e, 1)
                if e == 0:
                    for g in range(NT):
                        par = g % 2
                        st = p2st[:, 24 * par:24 * (par + 1)]
                        vector.wait_ge(mmr, g + 1)
                        vector.reduce_max(st[:, 0:1], ps_r2[par][:, :], axis=AX.X)
                        vector.drain()
                        vector.tensor_scalar(st[:, 1:2], st[:, 0:1], -1.0, None,
                                             ALU.mult).then_inc(r2s, 1)
                        vector.wait_ge(s2r, g + 1)
                        vector.reduce_sum(st[:, 2:3], st[:, 8:16], axis=AX.X)
                        vector.drain()
                        vector.reciprocal(st[:, 3:4], st[:, 2:3])
                        vector.tensor_scalar(disp[:, 8 * g:8 * (g + 1)], ps_r2[par][:, :],
                                             st[:, 0:1], None, ALU.is_ge)
                        vector.drain()
                        vector.tensor_scalar(disp[:, 8 * g:8 * (g + 1)],
                                             disp[:, 8 * g:8 * (g + 1)],
                                             st[:, 3:4], None, ALU.mult).then_inc(ev_r, 1)
                for g in range(NT):
                    idx = NT * e + g
                    vector.wait_ge(mme, idx + 1)
                    dcol = disp[:, 8 * g + e:8 * g + e + 1]
                    if e == 0:
                        vector.tensor_scalar(g_tm[:, ts(g, D)], ps_e[:, ts(g % 2, D)],
                                             dcol, None, ALU.mult).then_inc(ev_e, 1)
                    else:
                        vector.scalar_tensor_tensor(g_tm[:, ts(g, D)], ps_e[:, ts(g % 2, D)],
                                                    dcol, g_tm[:, ts(g, D)],
                                                    ALU.mult, ALU.add).then_inc(ev_e, 1)

        @blk.scalar
        def _(scalar):
            for g in range(NT):
                par = g % 2
                st = p2st[:, 24 * par:24 * (par + 1)]
                scalar.wait_ge(r2s, g + 1)
                scalar.activation(st[:, 8:16], ps_r2[par][:, :], AF.Exp,
                                  bias=st[:, 1:2], scale=1.0).then_inc(s2r, 1)
            for g in range(NT):
                scalar.wait_ge(ev_e, NT * (NEXP - 1) + g + 1)
                scalar.activation(g_tm[:, ts(g, D)], g_tm[:, ts(g, D)],
                                  AF.Silu, bias=0.0, scale=1.0)

        @blk.tensor
        def _(tensor):
            tensor.wait_ge(cast_e, 1)
            for g in range(NT):
                par = g % 2
                if g >= 2:
                    tensor.wait_ge(ev_r, g - 1)
                tensor.matmul(ps_r2[par][:, :], ones_r[0:1, 0:128], rb_s[0:1, :],
                              start=True, stop=False)
                for i in range(8):
                    mm = tensor.matmul(ps_r2[par][:, :],
                                       xn_fm[:, i * T + 128 * g:i * T + 128 * (g + 1)],
                                       rw_s[:, 8 * i:8 * (i + 1)],
                                       start=False, stop=(i == 7))
                mm.then_inc(mmr, 1)
            for e in range(NEXP):
                tensor.wait_ge(cast_e, 1 + 8 * (e + 1))
                tensor.wait_ge(ld_eb[e % 2], 16 * (e // 2 + 1))
                for g in range(NT):
                    idx = NT * e + g
                    if idx >= 2:
                        tensor.wait_ge(ev_e, idx - 1)
                    for half in range(2):
                        seg = ps_e[:, (g % 2) * D + 512 * half:(g % 2) * D + 512 * (half + 1)]
                        tensor.matmul(seg, ones_r[0:1, 0:128],
                                      eb_s[0:1, (e % 2) * D + 512 * half:(e % 2) * D + 512 * (half + 1)],
                                      start=True, stop=False)
                        for i in range(8):
                            mm = tensor.matmul(seg,
                                               xn_fm[:, i * T + 128 * g:i * T + 128 * (g + 1)],
                                               ew16[e % 2][:, D * i + 512 * half:D * i + 512 * (half + 1)],
                                               start=False, stop=(i == 7))
                    mm.then_inc(mme, 1)

    if upto <= 3:
        return c
    o_pair = q_pair

    NKC = S // 128
    NGR = S // 1024

    with ExitStack() as es:
        es.enter_context(nc.named_scope("p3"))
        def _sb(n, s, d):
            return es.enter_context(nc.sbuf_tensor(n, s, d))

        def _ps(n, s, d=F32):
            return es.enter_context(nc.psum_tensor(n, s, d))

        _sem = mk_sem_scope(nc, es)

        kpair = _sb("kpair", [128, S], BF16)
        vbuf = [_sb("vbuf0", [128, NKC * 65], BF16), _sb("vbuf1", [128, NKC * 65], BF16)]
        qaug = [_sb("qaug0", [65, T], BF16), _sb("qaug1", [65, T], BF16)]
        pt = _sb("pt", [128, 2 * 1024], BF16)
        zrow = _sb("zrow", [1, 1024], F32)
        zcol = _sb("zcol", [128, 8], F32)
        zb = _sb("zb", [64, 2 * 1024], F32)
        ones16 = _sb("p3_ones16", [128, 1], BF16)
        qsq16 = _sb("p3_qsq16", [128, T], BF16)
        nmax = _sb("p3_nmax", [1, 4], F32)
        negm_row = _sb("p3_negmr", [1, T], F32)
        negm16 = _sb("p3_negm16", [1, T], BF16)
        negm_all = _sb("p3_negma", [16, T], BF16)
        ps_big = _ps("ps_big", [128, 2048])
        ps_o = [_ps("ps_o0", [65, 1024]), _ps("ps_o1", [65, 1024])]
        ldkp = [_sem("p3_ldkp0"), _sem("p3_ldkp1")]
        ldka = [_sem("p3_ldka0"), _sem("p3_ldka1")]
        ldv = [_sem("p3_ldv0"), _sem("p3_ldv1")]
        qagv = _sem("p3_qagv")
        nmd2 = [_sem("p3_nmd0"), _sem("p3_nmd1")]
        mmS = _sem("p3_mmS")
        exS = _sem("p3_exS")
        mmO = _sem("p3_mmO")
        oev = _sem("p3_oev")
        zev = _sem("p3_zev")
        zbd = _sem("p3_zbd")
        p2dn = _sem("p3_p2dn")

        def g2idx(h, qb, kc):
            return (h * 2 + qb) * NKC + kc

        ZCH = 8

        sq_k = _sem("p3_sqk")
        qsq = _sem("p3_qsq")
        knr = _sem("p3_knr")
        kne = _sem("p3_kne")
        sqs = _sem("p3_sqs")
        nmv = _sem("p3_nmv")
        nmd = _sem("p3_nmd")
        zrv = _sem("p3_zrv")
        pvk = _sem("p3_pvk")
        blk = es.enter_context(nc.Block())

        @blk.sync
        def _(sync):
            for h in range(HEADS):
                if h >= 2:
                    sync.wait_ge(knr, 5 * (h - 1))
                sync.dma_start(out=kpair[64 * (h % 2):64 * (h % 2) + 64, :],
                               in_=k_sc[h]).then_inc(ldkp[h % 2], 16)

        @blk.scalar
        def _(scalar):
            def head_loads(h):
                sl = h % 2
                if h >= 2:
                    scalar.wait_ge(pvk, (2 * (h - 2) + 2) * NKC)
                scalar.dma_start(out=kaug[sl][0:64, :], in_=k_sc[h]).then_inc(ldka[sl], 16)
                scalar.dma_start(
                    out=vbuf[sl][:, :].rearrange("p (kc e) -> p kc e", kc=NKC),
                    in_=v_sc[h].rearrange("(kc p) e -> p kc e", p=128)).then_inc(ldv[sl], 16)

            for h in range(HEADS):
                scalar.wait_ge(knr, 5 * (h + 1))
                scalar.activation(negm_row[0:1, :], ps_big[0:1, 0:2048],
                                  AF.Sqrt).then_inc(sqs, 1)
                if h >= 1:
                    scalar.wait_ge(nmv, h)
            head_loads(0); head_loads(1)
            for h in range(HEADS):
                for qb in range(2):
                    for kc in range(NKC):
                        ci = g2idx(h, qb, kc)
                        scalar.wait_ge(mmS, ci + 1)
                        if ci >= 2:
                            scalar.wait_ge(pvk, ci - 1)
                        scalar.activation(pt[:, ts(kc % 2, 1024)], ps_big[:, ts(kc % 2, 1024)],
                                          AF.Exp, bias=0.0, scale=1.0).then_inc(exS, 1)
                if h + 2 < HEADS:
                    head_loads(h + 2)

        @blk.tensor
        def _(tensor):
            for h in range(HEADS):
                p, hh = h // 2, h % 2
                prng = slice(64 * (h % 2), 64 * (h % 2) + 64)
                tensor.wait_ge(sq_k, h + 1)
                for r in range(4):
                    if r >= 1:
                        tensor.wait_ge(kne, 4 * h + r)
                    elif h >= 1:
                        tensor.wait_ge(sqs, h)
                    for cch in range(4):
                        mm = tensor.matmul(ps_big[0:1, 512 * cch:512 * (cch + 1)],
                                           ones16[prng, 0:1],
                                           kpair[prng, 2048 * r + 512 * cch:
                                                 2048 * r + 512 * (cch + 1)],
                                           start=True, stop=True)
                    mm.then_inc(knr, 1)
                tensor.wait_ge(kne, 4 * (h + 1))
                tensor.wait_ge(qsq, p + 1)
                qrng = slice(64 * hh, 64 * hh + 64)
                for cch in range(4):
                    mm = tensor.matmul(ps_big[0:1, 512 * cch:512 * (cch + 1)],
                                       ones16[qrng, 0:1],
                                       qsq16[qrng, 512 * cch:512 * (cch + 1)],
                                       start=True, stop=True)
                mm.then_inc(knr, 1)
            tensor.wait_ge(sqs, HEADS)
            for p in range(8):
                for hh in range(2):
                    h = 2 * p + hh
                    sl = h % 2
                    tensor.wait_ge(ldka[sl], 16 * (h // 2 + 1))
                    tensor.wait_ge(ldv[sl], 16 * (h // 2 + 1))
                    tensor.wait_ge(qagv, h + 1)
                    tensor.wait_ge(nmd2[sl], 16 * (h // 2 + 1))
                    for qb in range(2):
                        oi = 2 * h + qb
                        if oi >= 2:
                            tensor.wait_ge(oev, oi - 1)
                        for kc in range(NKC + 1):
                            if kc < NKC:
                                ci = g2idx(h, qb, kc)
                                if ci >= 2:
                                    tensor.wait_ge(exS, ci - 1)
                                for half in range(2):
                                    mm = tensor.matmul(
                                        ps_big[:, 1024 * (kc % 2) + 512 * half:
                                               1024 * (kc % 2) + 512 * (half + 1)],
                                        kaug[sl][:, 128 * kc:128 * (kc + 1)],
                                        qaug[sl][:, 1024 * qb + 512 * half:
                                                 1024 * qb + 512 * (half + 1)],
                                        start=True, stop=True)
                                mm.then_inc(mmS, 1)
                            if kc >= 1:
                                cip = g2idx(h, qb, kc - 1)
                                tensor.wait_ge(exS, cip + 1)
                                for half in range(2):
                                    mm = tensor.matmul(
                                        ps_o[qb][:, 512 * half:512 * (half + 1)],
                                        vbuf[sl][:, 65 * (kc - 1):65 * kc],
                                        pt[:, 1024 * ((kc - 1) % 2) + 512 * half:
                                           1024 * ((kc - 1) % 2) + 512 * (half + 1)],
                                        start=(kc - 1 == 0), stop=(kc - 1 == NKC - 1))
                                mm.then_inc(pvk, 1)

        @blk.vector
        def _(vector):
            vector.memset(kaug[0][64:65, :], 1.0)
            vector.memset(kaug[1][64:65, :], 1.0)
            vector.memset(ones16[:, :], 1.0)
            for h in range(HEADS):
                p, hh = h // 2, h % 2
                prng = slice(64 * (h % 2), 64 * (h % 2) + 64)
                vector.wait_ge(ldkp[h % 2], 16 * (h // 2 + 1))
                vector.tensor_tensor(kpair[prng, :], kpair[prng, :], kpair[prng, :],
                                     ALU.mult).then_inc(sq_k, 1)
                if hh == 0:
                    if p >= 1:
                        vector.wait_ge(knr, 5 * 2 * p)
                    vector.tensor_tensor(qsq16[:, :], q_pair[p][:, :], q_pair[p][:, :],
                                         ALU.mult).then_inc(qsq, 1)
                for r in range(4):
                    vector.wait_ge(knr, 5 * h + r + 1)
                    vector.drain()
                    vector.reduce_max(nmax[0:1, 1:2], ps_big[0:1, 0:2048], axis=AX.X)
                    vector.drain()
                    if r == 0:
                        vector.tensor_copy(nmax[0:1, 0:1], nmax[0:1, 1:2]).then_inc(kne, 1)
                    else:
                        vector.tensor_tensor(nmax[0:1, 0:1], nmax[0:1, 0:1],
                                             nmax[0:1, 1:2], ALU.max).then_inc(kne, 1)
                vector.drain()
                vector.tensor_scalar(nmax[0:1, 2:3], nmax[0:1, 0:1], -1.0, None,
                                     ALU.mult)
                vector.drain()
                vector.wait_ge(sqs, h + 1)
                if h >= 1:
                    vector.wait_ge(nmd, 16 * h)
                vector.tensor_scalar(negm16[0:1, :], negm_row[0:1, :],
                                     nmax[0:1, 2:3], 40.0,
                                     ALU.mult, ALU.add).then_inc(nmv, 1)
            for p in range(8):
                for hh in range(2):
                    h = 2 * p + hh
                    sl = h % 2
                    if h >= 2:
                        vector.wait_ge(pvk, (2 * (h - 2) + 2) * NKC)
                    vector.tensor_copy(qaug[sl][0:64, :],
                                       q_pair[p][64 * hh:64 * (hh + 1), :]).then_inc(qagv, 1)
                vector.drain()
                for hh in range(2):
                    h = 2 * p + hh
                    for qb in range(2):
                        oi = 2 * h + qb
                        vector.wait_ge(pvk, (oi + 1) * NKC)
                        vector.tensor_copy(zrow[:, :], ps_o[qb][64:65, :]).then_inc(zev, 1)
                        vector.wait_ge(zbd, 16 * (ZCH * oi + 1))
                        vector.reciprocal(zcol[:, 0:8], zcol[:, 0:8]).then_inc(zrv, 1)
                        vector.wait_ge(zbd, 16 * (ZCH * oi + ZCH))
                        vector.tensor_tensor(o_pair[p][64 * hh:64 * (hh + 1),
                                                       1024 * qb:1024 * (qb + 1)],
                                             ps_o[qb][0:64, :],
                                             zb[:, ts(qb, 1024)], ALU.mult).then_inc(oev, 1)

        @blk.gpsimd
        def _(gpsimd):
            for h in range(HEADS):
                gpsimd.wait_ge(nmv, h + 1)
                gpsimd.dma_start(out=negm_all[h:h + 1, :],
                                 in_=negm16[0:1, :]).then_inc(nmd, 16)
            gpsimd.wait_ge(nmd, 16 * HEADS)
            for p in range(8):
                for hh in range(2):
                    h = 2 * p + hh
                    sl = h % 2
                    if h >= 2:
                        gpsimd.wait_ge(pvk, (2 * (h - 2) + 2) * NKC)
                    gpsimd.dma_start(out=qaug[sl][64:65, :],
                                     in_=negm_all[h:h + 1, :]).then_inc(nmd2[sl], 16)
                for hh in range(2):
                    h = 2 * p + hh
                    for qb in range(2):
                        oi = 2 * h + qb
                        base = 16 * ZCH * oi
                        gpsimd.wait_ge(zev, oi + 1)
                        gpsimd.dma_start(
                            out=zcol[:, 0:8],
                            in_=zrow[0:1, :].rearrange("a (pp j) -> a pp j", pp=128),
                        ).then_inc(zbd, 16)
                        gpsimd.wait_ge(zrv, oi + 1)
                        gpsimd.dma_start(
                            out=zb[0:1, 1024 * qb:1024 * (qb + 1)].rearrange("a (pp j) -> a pp j", pp=128),
                            in_=zcol[:, 0:8],
                        ).then_inc(zbd, 16)
                        k = 1
                        for d in range(6):
                            gpsimd.wait_ge(zbd, base + 16 * (2 + d))
                            gpsimd.dma_start(
                                out=zb[k:2 * k, 1024 * qb:1024 * (qb + 1)],
                                in_=zb[0:k, 1024 * qb:1024 * (qb + 1)],
                            ).then_inc(zbd, 16)
                            k *= 2

    if upto <= 4:
        return c

    with ExitStack() as es:
        es.enter_context(nc.named_scope("p4a"))
        def _sb(n, s, d):
            return es.enter_context(nc.sbuf_tensor(n, s, d))

        def _ps(n, s, d=F32):
            return es.enter_context(nc.psum_tensor(n, s, d))

        _sem = mk_sem_scope(nc, es)

        wo16 = _sb("wo16", [128, 8 * D], BF16)
        wstg = _sb("p4a_wstg", [128, 2 * D], F32)
        bo_s = _sb("bo_s", [1, D], F32)
        g2s = _sb("p4a_g2s", [128, 16], F32)
        xt_s = _sb("p4a_xt", [128, 2 * D], F32)
        st4 = _sb("p4a_st", [128, 2 * 16], F32)
        x2t = _sb("p4a_x2t", [128, 2 * D], BF16)
        ps_wo = _ps("ps_wo", [128, 2 * D])
        ps_tp4 = _ps("ps_tp4", [128, 1024], BF16)
        ldw = [_sem("p4a_ldw0"), _sem("p4a_ldw1")]
        ldb = _sem("p4a_ldb")
        ldxt = [_sem("p4a_ldxt0"), _sem("p4a_ldxt1")]
        cst = _sem("p4a_cst")
        mmo = _sem("p4a_mmo")
        h1v = _sem("p4a_h1v")
        s2v4 = _sem("p4a_s2v4")
        v2s4 = _sem("p4a_v2s4")
        s2x = _sem("p4a_s2x")
        ttp = _sem("p4a_ttp")
        vfm4 = _sem("p4a_vfm4")
        blk = es.enter_context(nc.Block())

        @blk.sync
        def _(sync):
            sync.dma_start(out=bo_s[:, :], in_=bo_r).then_inc(ldb, 16)
            sync.dma_start(out=g2s[:, 0:8], in_=g2_c).then_inc(ldb, 16)
            sync.dma_start(out=g2s[:, 8:16], in_=be2_c).then_inc(ldb, 16)
            for i in range(8):
                if i >= 2:
                    sync.wait_ge(cst, i - 1)
                sync.dma_start(out=wstg[:, ts(i % 2, D)],
                               in_=wo_h[ts(i, 128), :]).then_inc(ldw[i % 2], 16)
            for g in range(NT):
                if g >= 2:
                    sync.wait_ge(h1v, g - 1)
                sync.dma_start(out=xt_s[:, ts(g % 2, D)],
                               in_=x_full[ts(g, 128), :]).then_inc(ldxt[g % 2], 16)

        @blk.vector
        def _(vector):
            def evac4(g):
                vector.wait_ge(ttp, g + 1)
                for j in range(8):
                    ev = vector.tensor_scalar(
                        x2fm[:, j * T + 128 * g:j * T + 128 * (g + 1)],
                        ps_tp4[:, 128 * j:128 * (j + 1)],
                        g2s[:, j:j + 1], g2s[:, 8 + j:9 + j],
                        ALU.mult, ALU.add)
                ev.then_inc(vfm4, 1)

            vector.wait_ge(ldb, 48)
            for i in range(8):
                vector.wait_ge(ldw[i % 2], 16 * (i // 2 + 1))
                vector.tensor_copy(wo16[:, ts(i, D)], wstg[:, ts(i % 2, D)]).then_inc(cst, 1)
            for g in range(NT):
                par = g % 2
                sl = st4[:, 16 * par:16 * (par + 1)]
                if g >= 2:
                    vector.wait_ge(s2x, g - 1)
                vector.tensor_scalar(sl[:, 0:1], ln0[:, 2 * g:2 * g + 1], -1.0, None, ALU.mult)
                vector.drain()
                vector.tensor_tensor(sl[:, 0:1], sl[:, 0:1], ln0[:, 2 * g + 1:2 * g + 2], ALU.mult)
                vector.wait_ge(ldxt[par], 16 * (g // 2 + 1))
                vector.wait_ge(mmo, g + 1)
                vector.drain()
                vector.tensor_scalar(xt_s[:, ts(par, D)], xt_s[:, ts(par, D)],
                                     ln0[:, 2 * g + 1:2 * g + 2], sl[:, 0:1],
                                     ALU.mult, ALU.add)
                vector.drain()
                vector.tensor_tensor(h1_tm[:, ts(g, D)], ps_wo[:, ts(par, D)],
                                     xt_s[:, ts(par, D)], ALU.add)
                vector.drain()
                vector.bn_stats(sl[:, 2:8], h1_tm[:, g * D:g * D + 512])
                vector.bn_stats(sl[:, 8:14], h1_tm[:, g * D + 512:g * D + 1024])
                vector.drain()
                vector.bn_aggr(sl[:, 0:2], sl[:, 2:14])
                vector.drain()
                vector.tensor_scalar(sl[:, 2:3], sl[:, 1:2], EPS, None, ALU.add)
                vector.drain()
                vector.reciprocal(sl[:, 3:4], sl[:, 2:3]).then_inc(h1v, 1)
                vector.wait_ge(s2v4, g + 1)
                vector.tensor_scalar(sl[:, 5:6], sl[:, 0:1], -1.0, None, ALU.mult)
                vector.drain()
                vector.tensor_tensor(sl[:, 5:6], sl[:, 5:6], sl[:, 4:5],
                                     ALU.mult).then_inc(v2s4, 1)
                if g >= 1:
                    evac4(g - 1)
            evac4(NT - 1)

        @blk.scalar
        def _(scalar):
            for g in range(NT):
                par = g % 2
                sl = st4[:, 16 * par:16 * (par + 1)]
                scalar.wait_ge(h1v, g + 1)
                scalar.activation(sl[:, 4:5], sl[:, 3:4], AF.Sqrt).then_inc(s2v4, 1)
                scalar.wait_ge(v2s4, g + 1)
                if g >= 2:
                    scalar.wait_ge(ttp, g - 1)
                scalar.activation(x2t[:, ts(par, D)], h1_tm[:, ts(g, D)],
                                  AF.Identity, bias=sl[:, 5:6],
                                  scale=sl[:, 4:5]).then_inc(s2x, 1)

        @blk.tensor
        def _(tensor):
            tensor.wait_ge(ldb, 48)
            tensor.wait_ge(cst, 8)
            for g in range(NT):
                par = g % 2
                if g >= 2:
                    tensor.wait_ge(h1v, g - 1)
                for half in range(2):
                    seg = ps_wo[:, par * D + 512 * half:par * D + 512 * (half + 1)]
                    tensor.matmul(seg, ones_r[0:1, 0:128], bo_s[0:1, ts(half, 512)],
                                  start=True, stop=False)
                    for i in range(8):
                        mm = tensor.matmul(seg,
                                           o_pair[i][:, 128 * g:128 * (g + 1)],
                                           wo16[:, D * i + 512 * half:D * i + 512 * (half + 1)],
                                           start=False, stop=(i == 7))
                mm.then_inc(mmo, 1)
                tensor.wait_ge(s2x, g + 1)
                if g >= 1:
                    tensor.wait_ge(vfm4, g)
                for j in range(8):
                    tp = tensor.transpose(ps_tp4[:, 128 * j:128 * (j + 1)],
                                          x2t[:, D * par + 128 * j:D * par + 128 * (j + 1)],
                                          ident[:, :])
                tp.then_inc(ttp, 1)



    if upto <= 5:
        return c
    with ExitStack() as es:
        es.enter_context(nc.named_scope("p4b"))
        def _sb(n, s, d):
            return es.enter_context(nc.sbuf_tensor(n, s, d))

        def _ps(n, s, d=F32):
            return es.enter_context(nc.psum_tensor(n, s, d))

        _sem = mk_sem_scope(nc, es)

        w1stg = _sb("w1stg", [128, 2 * 1024], F32)
        w1c16 = _sb("w1c16", [128, 2 * 1024], BF16)
        w2stg = _sb("w2stg", [128, 32 * 128], F32)
        w2c16 = _sb("w2c16", [128, 32 * 128], BF16)
        b1c = _sb("b1c", [128, 32], F32)
        b2c = _sb("b2c", [128, 8], F32)
        ffn16 = _sb("ffn16", [128, 2 * 512], BF16)
        ps4b = _ps("ps4b", [128, 2 * 512])
        ps4c = _ps("ps4c", [128, 2 * 512])
        ps_tpf = _ps("ps_tpf", [128, 1024], BF16)
        ldb1 = _sem("p4b_ldb1")
        ldw1 = [_sem("p4b_ldw10"), _sem("p4b_ldw11")]
        ldw2 = [_sem("p4b_ldw20"), _sem("p4b_ldw21")]
        c1 = _sem("p4b_c1")
        c2 = _sem("p4b_c2")
        mh = _sem("p4b_mh")
        rl = _sem("p4b_rl")
        mf = _sem("p4b_mf")
        fe = _sem("p4b_fe")
        tf = _sem("p4b_tf")
        sv = _sem("p4b_sv")
        blk = es.enter_context(nc.Block())

        NI1 = 4 * 32
        NI2 = 4 * 8

        @blk.sync
        def _(sync):
            sync.dma_start(out=b1c[:, :], in_=b1_c_h).then_inc(ldb1, 16)
            sync.dma_start(out=b2c[:, :], in_=b2_c_h).then_inc(ldb1, 16)
            for blk_ in range(4):
                for oc1 in range(32):
                    it = blk_ * 32 + oc1
                    if it >= 2:
                        sync.wait_ge(c1, it - 1)
                    sync.dma_start(
                        out=w1stg[:, ts(it % 2, 1024)].rearrange("p (i cc) -> p i cc", i=8),
                        in_=w1_h[:, 128 * oc1:128 * (oc1 + 1)]
                            .rearrange("(i p) cc -> p i cc", p=128)).then_inc(ldw1[it % 2], 16)
                for oc2 in range(8):
                    it = blk_ * 8 + oc2
                    if it >= 1:
                        sync.wait_ge(c2, it)
                    sync.dma_start(
                        out=w2stg[:, :].rearrange("p (i cc) -> p i cc", i=32),
                        in_=w2_h[:, 128 * oc2:128 * (oc2 + 1)]
                            .rearrange("(i p) cc -> p i cc", p=128)).then_inc(ldw2[it % 2], 16)

        @blk.vector
        def _(vector):
            vector.wait_ge(ldb1, 32)
            for blk_ in range(4):
                for oc1 in range(32):
                    it = blk_ * 32 + oc1
                    vector.wait_ge(ldw1[it % 2], 16 * (it // 2 + 1))
                    if it >= 2:
                        vector.wait_ge(mh, it - 1)
                    vector.tensor_copy(w1c16[:, ts(it % 2, 1024)],
                                       w1stg[:, ts(it % 2, 1024)]).then_inc(c1, 1)
                for oc2 in range(8):
                    it = blk_ * 8 + oc2
                    vector.wait_ge(ldw2[it % 2], 16 * (it // 2 + 1))
                    if it >= 1:
                        vector.wait_ge(mf, it)
                    vector.tensor_copy(w2c16[:, :], w2stg[:, :]).then_inc(c2, 1)
                    vector.wait_ge(mf, it + 1)
                    vector.tensor_scalar(ffn16[:, ts(it % 2, 512)],
                                         ps4c[:, ts(it % 2, 512)],
                                         b2c[:, oc2:oc2 + 1], None,
                                         ALU.add).then_inc(fe, 1)
                    vector.wait_ge(tf, it + 1)
                    for t in range(4):
                        g = 4 * blk_ + t
                        tt = vector.tensor_tensor(
                            h1_tm[:, g * D + 128 * oc2:g * D + 128 * (oc2 + 1)],
                            ps_tpf[:, 128 * t:128 * (t + 1)],
                            h1_tm[:, g * D + 128 * oc2:g * D + 128 * (oc2 + 1)],
                            ALU.add)
                    tt.then_inc(sv, 1)

        @blk.scalar
        def _(scalar):
            scalar.wait_ge(ldb1, 32)
            for blk_ in range(4):
                for oc1 in range(32):
                    it = blk_ * 32 + oc1
                    scalar.wait_ge(mh, it + 1)
                    scalar.activation(hblk[:, 512 * oc1:512 * (oc1 + 1)],
                                      ps4b[:, ts(it % 2, 512)], AF.Relu,
                                      bias=b1c[:, oc1:oc1 + 1], scale=1.0).then_inc(rl, 1)

        @blk.tensor
        def _(tensor):
            for blk_ in range(4):
                for oc1 in range(32):
                    it = blk_ * 32 + oc1
                    tensor.wait_ge(c1, it + 1)
                    if it >= 2:
                        tensor.wait_ge(rl, it - 1)
                    seg = ps4b[:, ts(it % 2, 512)]
                    for ic in range(8):
                        mm = tensor.matmul(
                            seg, w1c16[:, (it % 2) * 1024 + 128 * ic:(it % 2) * 1024 + 128 * (ic + 1)],
                            x2fm[:, ic * T + 512 * blk_:ic * T + 512 * (blk_ + 1)],
                            start=(ic == 0), stop=(ic == 7))
                    mm.then_inc(mh, 1)
                for oc2 in range(8):
                    it = blk_ * 8 + oc2
                    tensor.wait_ge(c2, it + 1)
                    tensor.wait_ge(rl, 32 * (blk_ + 1))
                    if it >= 2:
                        tensor.wait_ge(fe, it - 1)
                    seg = ps4c[:, ts(it % 2, 512)]
                    for hc in range(32):
                        mm = tensor.matmul(
                            seg, w2c16[:, 128 * hc:128 * (hc + 1)],
                            hblk[:, 512 * hc:512 * (hc + 1)],
                            start=(hc == 0), stop=(hc == 31))
                    mm.then_inc(mf, 1)
                    tensor.wait_ge(fe, it + 1)
                    if it >= 1:
                        tensor.wait_ge(sv, it)
                    for t in range(4):
                        tp = tensor.transpose(
                            ps_tpf[:, 128 * t:128 * (t + 1)],
                            ffn16[:, 512 * (it % 2) + 128 * t:512 * (it % 2) + 128 * (t + 1)],
                            ident[:, :])
                    tp.then_inc(tf, 1)

    if upto <= 6:
        return c
    with ExitStack() as es:
        es.enter_context(nc.named_scope("p5"))
        def _sb(n, s, d):
            return es.enter_context(nc.sbuf_tensor(n, s, d))

        def _ps(n, s, d=F32):
            return es.enter_context(nc.psum_tensor(n, s, d))

        _sem = mk_sem_scope(nc, es)

        xts = _sb("p5_xt", [128, 2 * D], F32)
        x1t = _sb("p5_x1t", [128, 2 * D], F32)
        sg = _sb("p5_sg", [128, 2 * D], F32)
        st5 = _sb("p5_st", [128, 2 * 16], F32)
        xn1t = _sb("p5_xn1t", [128, 2 * D], BF16)
        ps_tp5 = _ps("ps_tp5", [128, 1024], BF16)
        ldx5 = [_sem("p5_ldx0"), _sem("p5_ldx1")]
        vst5 = _sem("p5_vst")
        s2v5 = _sem("p5_s2v")
        ttp5 = _sem("p5_ttp")
        vfm5 = _sem("p5_vfm")
        stx1 = [_sem("p5_stx1a"), _sem("p5_stx1b")]
        blk = es.enter_context(nc.Block())

        @blk.sync
        def _(sync):
            for g in range(NT):
                if g >= 2:
                    sync.wait_ge(vst5, 2 * (g - 1))
                sync.dma_start(out=xts[:, ts(g % 2, D)],
                               in_=x_full[ts(g, 128), :]).then_inc(ldx5[g % 2], 16)

        @blk.vector
        def _(vector):
            for g in range(NT):
                par = g % 2
                sl = st5[:, 16 * par:16 * (par + 1)]
                vector.wait_ge(ldx5[par], 16 * (g // 2 + 1))
                if g >= 2:
                    vector.wait_ge(stx1[g % 2], 16 * (g // 2))
                    vector.wait_ge(s2v5, 2 * (g - 1))
                vector.tensor_tensor(sg[:, ts(par, D)], h1_tm[:, ts(g, D)],
                                     g_tm[:, ts(g, D)], ALU.mult)
                vector.drain()
                vector.tensor_tensor(x1t[:, ts(par, D)], sg[:, ts(par, D)],
                                     xts[:, ts(par, D)], ALU.add)
                vector.drain()
                vector.bn_stats(sl[:, 2:8], x1t[:, par * D:par * D + 512])
                vector.bn_stats(sl[:, 8:14], x1t[:, par * D + 512:par * D + 1024])
                vector.drain()
                vector.bn_aggr(sl[:, 0:2], sl[:, 2:14])
                vector.drain()
                vector.tensor_copy(ln1[:, 2 * g:2 * g + 1], sl[:, 0:1])
                vector.tensor_scalar(sl[:, 2:3], sl[:, 1:2], EPS, None, ALU.add)
                vector.drain()
                vector.reciprocal(sl[:, 3:4], sl[:, 2:3]).then_inc(vst5, 1)
                vector.wait_ge(s2v5, 2 * g + 1)
                vector.tensor_scalar(sl[:, 5:6], sl[:, 0:1], -1.0, None, ALU.mult)
                vector.drain()
                vector.tensor_tensor(sl[:, 5:6], sl[:, 5:6], sl[:, 4:5],
                                     ALU.mult).then_inc(vst5, 1)
                if g >= 1:
                    vector.wait_ge(ttp5, g)
                    vector.tensor_copy(
                        xn1_fm[:, :].rearrange("p (j cc) -> p j cc", j=8)[:, :, 128 * (g - 1):128 * g],
                        ps_tp5[:, :].rearrange("p (j cc) -> p j cc", j=8),
                    ).then_inc(vfm5, 1)
            vector.wait_ge(ttp5, NT)
            vector.tensor_copy(
                xn1_fm[:, :].rearrange("p (j cc) -> p j cc", j=8)[:, :, 128 * (NT - 1):128 * NT],
                ps_tp5[:, :].rearrange("p (j cc) -> p j cc", j=8),
            ).then_inc(vfm5, 1)

        @blk.scalar
        def _(scalar):
            for g in range(NT):
                par = g % 2
                sl = st5[:, 16 * par:16 * (par + 1)]
                scalar.wait_ge(vst5, 2 * g + 1)
                scalar.activation(sl[:, 4:5], sl[:, 3:4], AF.Sqrt)
                scalar.drain()
                scalar.activation(ln1[:, 2 * g + 1:2 * g + 2], sl[:, 3:4],
                                  AF.Sqrt).then_inc(s2v5, 1)
                scalar.wait_ge(vst5, 2 * g + 2)
                if g >= 2:
                    scalar.wait_ge(ttp5, g - 1)
                scalar.activation(xn1t[:, ts(par, D)], x1t[:, ts(par, D)],
                                  AF.Identity, bias=sl[:, 5:6],
                                  scale=sl[:, 4:5]).then_inc(s2v5, 1)

        @blk.tensor
        def _(tensor):
            for g in range(NT):
                par = g % 2
                tensor.wait_ge(s2v5, 2 * g + 2)
                if g >= 1:
                    tensor.wait_ge(vfm5, g)
                for j in range(8):
                    tp = tensor.transpose(ps_tp5[:, 128 * j:128 * (j + 1)],
                                          xn1t[:, D * par + 128 * j:D * par + 128 * (j + 1)],
                                          ident[:, :])
                tp.then_inc(ttp5, 1)

        @blk.gpsimd
        def _(gpsimd):
            for g in range(NT):
                gpsimd.wait_ge(vst5, 2 * g + 1)
                gpsimd.dma_start(out=x1_sc[ts(g, 128), :],
                                 in_=x1t[:, ts(g % 2, D)]).then_inc(stx1[g % 2], 16)
            gpsimd.wait_ge(stx1[0], 16 * (NT // 2))
            gpsimd.wait_ge(stx1[1], 16 * (NT // 2))
            gpsimd.wait_ge(vfm5, NT)
            gpsimd.dma_start(out=xn1_sc, in_=xn1_fm[:, :]).then_inc(stx1[0], 16)
            gpsimd.wait_ge(stx1[0], 16 * (NT // 2 + 1))


    if upto <= 7:
        return c
    enc_w16 = at("enc_w16", [128, 8 * L1_HID], BF16, OFF_A)
    dec_w16 = at("dec_w16", [128, 16 * D], BF16, OFF_C)
    spk_blk = at("spk_blk", [128, 16 * 512], BF16, OFF_D)
    rat_blk = at("rat_blk", [128, 16 * 512], BF16, OFF_D + 16384)
    g1_tm = at("g1_tm", [128, NT * D], BF16, OFF_B)

    with ExitStack() as es:
        es.enter_context(nc.named_scope("p6"))
        def _sb(n, s, d):
            return es.enter_context(nc.sbuf_tensor(n, s, d))

        def _ps(n, s, d=F32):
            return es.enter_context(nc.psum_tensor(n, s, d))

        _sem = mk_sem_scope(nc, es)

        wst6 = _sb("p6_wst", [128, 2 * D], F32)
        s2c16 = _sb("p6_s2c16", [128, 16 * 64], BF16)
        s2cst = _sb("p6_s2cst", [128, 16 * 64], F32)
        c2s16 = _sb("p6_c2s16", [64, L1_HID], BF16)
        rw16 = _sb("p6_rw16", [64, 72], BF16)
        rows6 = _sb("p6_rows6", [128, 3136], F32)
        onesf = _sb("p6_onesf", [128, 512], F32)
        cont16 = _sb("p6_cont", [64, T], BF16)
        rh16 = _sb("p6_rh", [64, T], BF16)
        st6 = _sb("p6_st6", [128, 2 * 24], F32)
        comb_tm = _sb("p6_combtm", [128, 128], BF16)
        comb_tr16 = _sb("p6_combtr", [128, 128], BF16)
        combbc = _sb("p6_combbc", [64, T], BF16)
        e116 = _sb("p6_e116", [64, EXPH], BF16)
        e216 = _sb("p6_e216", [128, 8 * 64], BF16)
        ehc = _sb("p6_ehc", [128, 2 * T], BF16)
        eb1c = _sb("p6_eb1c", [128, 8], F32)
        eb2c = _sb("p6_eb2c", [64, 1], F32)
        encbc = _sb("p6_encbc", [128, 16], F32)
        c2sbc = _sb("p6_c2sbc", [128, 16], F32)
        moe16 = _sb("p6_moe16", [64, T], BF16)
        dect = _sb("p6_dect", [128, 2 * D], F32)
        moet = dect[0:64, :]
        lngb = _sb("p6_lngb", [128, 2 * D], F32)
        c2sst = at("p6_c2sst", [64, L1_HID], F32, OFF_B)
        e1st = at("p6_e1st", [64, EXPH], F32, OFF_B + 8192)
        e2st = at("p6_e2st", [128, 8 * 64], F32, OFF_B + 12288)
        rwst = at("p6_rwst", [64, 72], F32, OFF_B + 14336)
        moe = at("p6_moe", [64, T], F32, OFF_B + 16384)
        ps61 = _ps("p6_ps61", [128, 2 * 512])
        ps62 = _ps("p6_ps62", [128, 2048])
        ps63 = _ps("p6_ps63", [128, 512])
        ld6 = [_sem("p6_ld0"), _sem("p6_ld1")]
        lds = _sem("p6_lds")
        c6 = _sem("p6_c6")
        ce6 = _sem("p6_ce6")
        menc = _sem("p6_menc"); senc = _sem("p6_senc")
        ms2c = _sem("p6_ms2c"); es2c = _sem("p6_es2c")
        mrh = _sem("p6_mrh"); erh = _sem("p6_erh")
        mlg = _sem("p6_mlg"); rtv = _sem("p6_rtv"); rts = _sem("p6_rts")
        mct = _sem("p6_mct"); ect = _sem("p6_ect")
        cbc = _sem("p6_cbc")
        meh = _sem("p6_meh"); seh = _sem("p6_seh")
        meo = _sem("p6_meo"); emo = _sem("p6_emo")
        mcs = _sem("p6_mcs"); scs = _sem("p6_scs")
        mdc = _sem("p6_mdc"); vdc = _sem("p6_vdc"); sdc = _sem("p6_sdc")
        dbg6 = _sem("p6_dbg")
        tap_spk = c.tap("t_spk", [128, 16 * 512], BF16)
        tap_cont = c.tap("t_cont", [64, 4 * 512], BF16)
        tap_comb = c.tap("t_comb", [128, 128], BF16)
        tap_moe = c.tap("t_moe", [64, T], BF16)
        blk = es.enter_context(nc.Block())

        NLD = 9

        @blk.sync
        def _(sync):
            sync.dma_start(out=encbc[:, :], in_=enc_bc_h).then_inc(lds, 16)
            sync.dma_start(out=c2sbc[:, :], in_=c2s_bc_h).then_inc(lds, 16)
            sync.dma_start(out=rows6[64:65, 0:D], in_=dec_b_r).then_inc(lds, 16)
            sync.dma_start(out=rows6[64:65, 1024:1088], in_=s2c_b_r).then_inc(lds, 16)
            sync.dma_start(out=rows6[64:65, 1088:1152], in_=r_b1_r).then_inc(lds, 16)
            sync.dma_start(out=rows6[64:65, 1152:1160], in_=r_b2_r).then_inc(lds, 16)
            sync.dma_start(out=s2cst[:, :].rearrange("p (i e) -> p i e", i=16),
                           in_=s2c_w_h.rearrange("(i p) e -> p i e", p=128)).then_inc(lds, 16)
            sync.dma_start(out=lngb[:, 0:D], in_=ln_g_bc_h).then_inc(lds, 16)
            sync.dma_start(out=lngb[:, D:2 * D], in_=ln_b_bc_h).then_inc(lds, 16)
            for i in range(16):
                if i >= 2:
                    sync.wait_ge(c6, i - 1)
                sync.dma_start(out=wst6[:, ts(i % 2, D)],
                               in_=enc_w_h[ts(i // 2, 128), 1024 * (i % 2):1024 * (i % 2) + 1024]
                               ).then_inc(ld6[i % 2], 16)
            sync.wait_ge(menc, 64)
            sync.wait_ge(lds, 16 * NLD)
            sync.dma_start(out=c2sst[:, :], in_=c2s_w_h).then_inc(lds, 16)
            sync.dma_start(out=rwst[:, 0:64], in_=r_w1_h).then_inc(lds, 16)
            sync.dma_start(out=rwst[:, 64:72], in_=r_w2_h).then_inc(lds, 16)
            for e in range(NEXP):
                sync.wait_ge(lds, 16 * (NLD + 3 + 4 * e))
                if e >= 1:
                    sync.wait_ge(meo, 8 * e)
                    sync.wait_ge(emo, 2 * e)
                sync.dma_start(out=e1st[:, :], in_=e_w1_h[e]).then_inc(lds, 16)
                sync.dma_start(out=e2st[:, :].rearrange("p (i e2) -> p i e2", i=8),
                               in_=e_w2_h[e].rearrange("(i p) e2 -> p i e2", p=128)).then_inc(lds, 16)
                sync.dma_start(out=eb1c[:, :], in_=e_b1_c_h[e]).then_inc(lds, 16)
                sync.dma_start(out=eb2c[:, :], in_=e_b2_c_h[e]).then_inc(lds, 16)
            for i in range(16):
                k = 16 + i
                sync.wait_ge(c6, k - 1)
                sync.dma_start(out=wst6[:, ts(k % 2, D)],
                               in_=dec_w_h[ts(i, 128), :]).then_inc(ld6[k % 2], 16)

        @blk.vector
        def _(vector):
            vector.memset(onesf[:, :], 1.0)
            vector.wait_ge(lds, 16 * NLD)
            vector.tensor_copy(s2c16[:, :], s2cst[:, :])
            vector.drain()
            for i in range(16):
                vector.wait_ge(ld6[i % 2], 16 * (i // 2 + 1))
                vector.tensor_copy(
                    enc_w16[:, L1_HID * (i // 2) + 1024 * (i % 2):
                            L1_HID * (i // 2) + 1024 * (i % 2) + 1024],
                    wst6[:, ts(i % 2, D)]).then_inc(c6, 1)
            for b in range(4):
                vector.wait_ge(ms2c, b + 1)
                if b == 3:
                    vector.wait_ge(lds, 16 * (NLD + 3))
                    vector.tensor_copy(c2s16[:, :], c2sst[:, :])
                    vector.tensor_copy(rw16[:, :], rwst[:, :])
                vector.tensor_copy(cont16[:, 512 * b:512 * (b + 1)],
                                   ps63[0:64, :]).then_inc(es2c, 1)
            for g in range(NT):
                par = g % 2
                sl = st6[:, 24 * par:24 * (par + 1)]
                vector.wait_ge(mlg, g + 1)
                seg = ps61[:, (g % 2) * 512:(g % 2) * 512 + 8]
                vector.reduce_max(sl[:, 0:1], seg, axis=AX.X)
                vector.drain()
                vector.tensor_scalar(sl[:, 1:2], sl[:, 0:1], -1.0, None,
                                     ALU.mult).then_inc(rtv, 1)
                vector.wait_ge(rts, g + 1)
                vector.reduce_sum(sl[:, 2:3], sl[:, 8:16], axis=AX.X)
                vector.drain()
                vector.reciprocal(sl[:, 3:4], sl[:, 2:3])
                vector.drain()
                vector.tensor_scalar(sl[:, 8:16], sl[:, 8:16], sl[:, 3:4], None, ALU.mult)
                vector.drain()
                vector.reduce_max(sl[:, 4:5], sl[:, 8:16], axis=AX.X)
                vector.drain()
                vector.tensor_scalar(sl[:, 16:24], sl[:, 8:16], sl[:, 4:5], None, ALU.is_ge)
                vector.drain()
                vector.tensor_tensor(sl[:, 16:24], sl[:, 16:24], sl[:, 8:16], ALU.mult)
                vector.drain()
                vector.tensor_tensor(sl[:, 8:16], sl[:, 8:16], sl[:, 16:24], ALU.subtract)
                vector.drain()
                vector.reduce_max(sl[:, 5:6], sl[:, 8:16], axis=AX.X)
                vector.drain()
                vector.tensor_scalar(sl[:, 8:16], sl[:, 8:16], sl[:, 5:6], None, ALU.is_ge)
                vector.drain()
                vector.tensor_scalar(sl[:, 8:16], sl[:, 8:16], sl[:, 5:6], None, ALU.mult)
                vector.drain()
                vector.tensor_tensor(sl[:, 16:24], sl[:, 16:24], sl[:, 8:16], ALU.add)
                vector.tensor_tensor(sl[:, 6:7], sl[:, 4:5], sl[:, 5:6], ALU.add)
                vector.drain()
                vector.reciprocal(sl[:, 7:8], sl[:, 6:7])
                vector.drain()
                vector.tensor_scalar(
                    comb_tm[:, :].rearrange("p (e gg) -> p e gg", e=8)[:, :, g:g + 1],
                    sl[:, 16:24].unsqueeze(2),
                    sl[:, 7:8], None, ALU.mult).then_inc(rtv, 1)
            vector.wait_ge(mct, 1)
            vector.tensor_copy(comb_tr16[:, :], ps63[:, 0:64].bitcast(BF16)).then_inc(ect, 1)
            for e in range(NEXP):
                vector.wait_ge(lds, 16 * (NLD + 3 + 4 * (e + 1)))
                if e >= 1:
                    vector.wait_ge(meo, 8 * e)
                vector.tensor_copy(e116[:, :], e1st[:, :])
                vector.tensor_copy(e216[:, :], e2st[:, :]).then_inc(ce6, 1)
                vector.wait_ge(meo, 8 * (e + 1))
                vector.wait_ge(cbc, 16 * (7 * e + 7))
                vector.scalar_tensor_tensor(moet[:, :], ps62[0:64, 0:2048],
                                            eb2c[0:64, 0:1], combbc[:, :],
                                            ALU.add, ALU.mult)
                vector.drain()
                if e == 0:
                    vector.tensor_copy(moe[:, :], moet[:, :]).then_inc(emo, 2)
                else:
                    vector.tensor_tensor(moe[:, :], moe[:, :], moet[:, :],
                                         ALU.add).then_inc(emo, 2)
            vector.drain()
            vector.tensor_copy(moe16[:, :], moe[:, :]).then_inc(emo, 1)
            for i in range(16, 32):
                vector.wait_ge(ld6[i % 2], 16 * (i // 2 + 1))
                vector.tensor_copy(dec_w16[:, ts(i - 16, D)],
                                   wst6[:, ts(i % 2, D)]).then_inc(c6, 1)
            for g in range(NT):
                par = g % 2
                sl = st6[:, 24 * par:24 * (par + 1)]
                vector.wait_ge(sdc, 2 * g + 1)
                vector.bn_stats(sl[:, 8:14], dect[:, par * D:par * D + 512])
                vector.bn_stats(sl[:, 14:20], dect[:, par * D + 512:par * D + 1024])
                vector.drain()
                vector.bn_aggr(sl[:, 0:2], sl[:, 8:20])
                vector.drain()
                vector.tensor_scalar(sl[:, 2:3], sl[:, 1:2], EPS, None, ALU.add)
                vector.drain()
                vector.reciprocal(sl[:, 3:4], sl[:, 2:3]).then_inc(vdc, 1)
                vector.wait_ge(sdc, 2 * g + 2)
                vector.tensor_scalar(sl[:, 5:6], sl[:, 0:1], -1.0, None, ALU.mult)
                vector.drain()
                vector.tensor_tensor(sl[:, 5:6], sl[:, 5:6], sl[:, 4:5], ALU.mult)
                vector.drain()
                vector.tensor_scalar(dect[:, ts(par, D)], dect[:, ts(par, D)],
                                     sl[:, 4:5], sl[:, 5:6], ALU.mult, ALU.add)
                vector.drain()
                vector.tensor_tensor(dect[:, ts(par, D)], dect[:, ts(par, D)],
                                     lngb[:, 0:D], ALU.mult)
                vector.drain()
                vector.tensor_tensor(dect[:, ts(par, D)], dect[:, ts(par, D)],
                                     lngb[:, D:2 * D], ALU.add).then_inc(vdc, 1)

        @blk.scalar
        def _(scalar):
            for b in range(4):
                for oc in range(16):
                    gi = b * 16 + oc
                    scalar.wait_ge(menc, gi + 1)
                    if oc == 0:
                        scalar.wait_ge(ms2c, b)
                    scalar.activation(spk_blk[:, 512 * oc:512 * (oc + 1)],
                                      ps61[:, ts(gi % 2, 512)], AF.Sigmoid,
                                      bias=encbc[:, oc:oc + 1], scale=1.0).then_inc(senc, 1)
            for b in range(4):
                scalar.wait_ge(mrh, b + 1)
                scalar.activation(rh16[:, 512 * b:512 * (b + 1)], ps63[0:64, :],
                                  AF.Relu, bias=0.0, scale=1.0).then_inc(erh, 1)
            for g in range(NT):
                par = g % 2
                sl = st6[:, 24 * par:24 * (par + 1)]
                scalar.wait_ge(rtv, 2 * g + 1)
                scalar.activation(sl[:, 8:16], ps61[:, (g % 2) * 512:(g % 2) * 512 + 8],
                                  AF.Exp, bias=sl[:, 1:2], scale=1.0).then_inc(rts, 1)
            for e in range(NEXP):
                for hc in range(8):
                    for tb in range(4):
                        gi = (e * 8 + hc) * 4 + tb
                        scalar.wait_ge(meh, gi + 1)
                        if tb == 0 and e * 8 + hc >= 2:
                            scalar.wait_ge(meo, e * 8 + hc - 1)
                        scalar.activation(
                            ehc[:, ((e * 8 + hc) % 2) * T + 512 * tb:
                                ((e * 8 + hc) % 2) * T + 512 * (tb + 1)],
                            ps61[:, ts(gi % 2, 512)], AF.Relu,
                            bias=eb1c[:, hc:hc + 1], scale=1.0).then_inc(seh, 1)
            for b in range(4):
                for oc in range(16):
                    gi = b * 16 + oc
                    scalar.wait_ge(mcs, gi + 1)
                    if oc == 0:
                        scalar.wait_ge(mdc, 4 * b)
                    scalar.activation(rat_blk[:, 512 * oc:512 * (oc + 1)],
                                      ps61[:, ts(gi % 2, 512)], AF.Sigmoid,
                                      bias=c2sbc[:, oc:oc + 1], scale=1.0).then_inc(scs, 1)
                for t in range(4):
                    g = 4 * b + t
                    par = g % 2
                    sl = st6[:, 24 * par:24 * (par + 1)]
                    scalar.wait_ge(mdc, g + 1)
                    scalar.activation(dect[:, ts(par, D)], ps62[:, ts(g % 2, 1024)],
                                      AF.Sigmoid, bias=0.0, scale=1.0).then_inc(sdc, 1)
                    scalar.wait_ge(vdc, 2 * g + 1)
                    scalar.activation(sl[:, 4:5], sl[:, 3:4], AF.Sqrt).then_inc(sdc, 1)
                    scalar.wait_ge(vdc, 2 * g + 2)
                    scalar.activation(g1_tm[:, ts(g, D)], dect[:, ts(par, D)],
                                      AF.Silu, bias=0.0, scale=1.0)

        @blk.tensor
        def _(tensor):
            tensor.wait_ge(lds, 16 * NLD)
            tensor.wait_ge(c6, 16)
            for b in range(4):
                for oc in range(16):
                    gi = b * 16 + oc
                    if gi >= 2:
                        tensor.wait_ge(senc, gi - 1)
                    seg = ps61[:, ts(gi % 2, 512)]
                    for ic in range(8):
                        mm = tensor.matmul(
                            seg, enc_w16[:, L1_HID * ic + 128 * oc:L1_HID * ic + 128 * (oc + 1)],
                            xn1_fm[:, ic * T + 512 * b:ic * T + 512 * (b + 1)],
                            start=(ic == 0), stop=(ic == 7))
                    mm.then_inc(menc, 1)
                tensor.wait_ge(senc, 16 * (b + 1))
                if b >= 1:
                    tensor.wait_ge(es2c, b)
                seg = ps63[0:64, :]
                tensor.matmul(seg, rows6[64:65, 1024:1088], onesf[64:65, :],
                              start=True, stop=False)
                for ic in range(16):
                    mm = tensor.matmul(seg, s2c16[:, 64 * ic:64 * (ic + 1)],
                                       spk_blk[:, 512 * ic:512 * (ic + 1)],
                                       start=False, stop=(ic == 15))
                mm.then_inc(ms2c, 1)
            tensor.wait_ge(es2c, 4)
            for b in range(4):
                if b >= 1:
                    tensor.wait_ge(erh, b)
                seg = ps63[0:64, :]
                tensor.matmul(seg, rows6[64:65, 1088:1152], onesf[64:65, :],
                              start=True, stop=False)
                mm = tensor.matmul(seg, rw16[:, 0:64], cont16[:, 512 * b:512 * (b + 1)],
                                   start=False, stop=True)
                mm.then_inc(mrh, 1)
            tensor.wait_ge(erh, 4)
            for g in range(NT):
                if g >= 2:
                    tensor.wait_ge(rts, g - 1)
                seg = ps61[:, (g % 2) * 512:(g % 2) * 512 + 8]
                tensor.matmul(seg, onesf[64:65, 0:128], rows6[64:65, 1152:1160],
                              start=True, stop=False)
                mm = tensor.matmul(seg, rh16[:, 128 * g:128 * (g + 1)], rw16[:, 64:72],
                                   start=False, stop=True)
                mm.then_inc(mlg, 1)
            tensor.wait_ge(rtv, 2 * NT)
            tensor.transpose(ps63[:, 0:64].bitcast(BF16), comb_tm[:, :],
                             ident[:, :]).then_inc(mct, 1)
            tensor.wait_ge(rts, NT)
            for e in range(NEXP):
                tensor.wait_ge(ce6, e + 1)
                if e >= 1:
                    tensor.wait_ge(emo, 2 * e)
                for hc in range(8):
                    u = e * 8 + hc
                    for tb in range(4):
                        gi = u * 4 + tb
                        if gi >= 2:
                            tensor.wait_ge(seh, gi - 1)
                        seg = ps61[:, ts(gi % 2, 512)]
                        mm = tensor.matmul(seg, e116[:, 128 * hc:128 * (hc + 1)],
                                           cont16[:, 512 * tb:512 * (tb + 1)],
                                           start=True, stop=True)
                        mm.then_inc(meh, 1)
                    tensor.wait_ge(seh, 4 * (u + 1))
                    for q4 in range(4):
                        mm = tensor.matmul(
                            ps62[0:64, 512 * q4:512 * (q4 + 1)],
                            e216[:, 64 * hc:64 * (hc + 1)],
                            ehc[:, (u % 2) * T + 512 * q4:(u % 2) * T + 512 * (q4 + 1)],
                            start=(hc == 0), stop=(hc == 7))
                    mm.then_inc(meo, 1)
            tensor.wait_ge(emo, 17)
            tensor.wait_ge(c6, 32)
            for b in range(4):
                for oc in range(16):
                    gi = b * 16 + oc
                    if gi >= 2:
                        tensor.wait_ge(scs, gi - 1)
                    seg = ps61[:, ts(gi % 2, 512)]
                    mm = tensor.matmul(seg, c2s16[:, 128 * oc:128 * (oc + 1)],
                                       moe16[:, 512 * b:512 * (b + 1)],
                                       start=True, stop=True)
                    mm.then_inc(mcs, 1)
                tensor.wait_ge(scs, 16 * (b + 1))
                for t in range(4):
                    g = 4 * b + t
                    if g >= 2:
                        tensor.wait_ge(sdc, 2 * (g - 1) + 1)
                    for half in range(2):
                        sseg = ps62[:, 1024 * (g % 2) + 512 * half:
                                    1024 * (g % 2) + 512 * (half + 1)]
                        tensor.matmul(sseg,
                                      onesf[64:65, 0:128],
                                      rows6[64:65, 512 * half:512 * (half + 1)],
                                      start=True, stop=False)
                        for ic in range(16):
                            mm = tensor.matmul(
                                sseg, rat_blk[:, 512 * ic + 128 * t:512 * ic + 128 * (t + 1)],
                                dec_w16[:, D * ic + 512 * half:D * ic + 512 * (half + 1)],
                                start=False, stop=(ic == 15))
                    mm.then_inc(mdc, 1)

        @blk.gpsimd
        def _(gpsimd):
            for e in range(NEXP):
                gpsimd.wait_ge(ect, 1)
                if e >= 1:
                    gpsimd.wait_ge(emo, 2 * e)
                gpsimd.dma_start(out=combbc[0:1, :],
                                 in_=comb_tr16[16 * e:16 * (e + 1), :]).then_inc(cbc, 16)
                k = 1
                for dd in range(6):
                    gpsimd.wait_ge(cbc, 16 * (7 * e + 1 + dd))
                    gpsimd.dma_start(out=combbc[k:2 * k, :],
                                     in_=combbc[0:k, :]).then_inc(cbc, 16)
                    k *= 2
            ndbg = 0
            if tap_spk is not None:
                gpsimd.wait_ge(senc, 64)
                gpsimd.dma_start(out=tap_spk, in_=spk_blk[:, :]).then_inc(dbg6, 16); ndbg += 1
            if tap_cont is not None:
                gpsimd.wait_ge(es2c, 4)
                gpsimd.dma_start(out=tap_cont, in_=cont16[:, :]).then_inc(dbg6, 16); ndbg += 1
            if tap_comb is not None:
                gpsimd.wait_ge(rtv, 2 * NT)
                gpsimd.dma_start(out=tap_comb, in_=comb_tm[:, :]).then_inc(dbg6, 16); ndbg += 1
            if tap_moe is not None:
                gpsimd.wait_ge(emo, 17)
                gpsimd.dma_start(out=tap_moe, in_=moe16[:, :]).then_inc(dbg6, 16); ndbg += 1
            if ndbg:
                gpsimd.wait_ge(dbg6, 16 * ndbg)

    if upto <= 8:
        return c
    s1_tm = at("s1_tm", [128, NT * D], BF16, OFF_D)
    proj16g = c.sb("proj16g", [128, 8 * 128], BF16)

    with ExitStack() as es:
        es.enter_context(nc.named_scope("p7a"))
        def _sb(n, s, d):
            return es.enter_context(nc.sbuf_tensor(n, s, d))

        def _ps(n, s, d=F32):
            return es.enter_context(nc.psum_tensor(n, s, d))

        _sem = mk_sem_scope(nc, es)
        wc16a = at("p7_wc16a", [128, 16 * 1024], BF16, OFF_A)
        wv16b = at("p7_wv16b", [128, 8 * D], BF16, OFF_C)
        xblk = _sb("p7_xblk", [128, 2 * 4096], BF16)
        wcst = _sb("p7_wcst", [128, 2 * 1024], F32)
        wvst = _sb("p7_wvst", [128, 2 * D], F32)
        prst = _sb("p7_prst", [128, 8 * 128], F32)
        qst = _sb("p7_qst", [128, 2 * 512], BF16)
        vstg = _sb("p7_vstg", [128, 2 * 1024], BF16)
        ps7a = _ps("p7_psa", [128, 2 * 512])
        ps7v = _ps("p7_psv", [128, 2 * 1024])
        ldx7 = [_sem("p7_ldx0"), _sem("p7_ldx1")]
        ldq7 = [_sem("p7_ldq0"), _sem("p7_ldq1")]
        ldr7 = [_sem("p7_ldr0"), _sem("p7_ldr1")]
        lds7 = _sem("p7_lds")
        cq7 = _sem("p7_cq"); cr7 = _sem("p7_cr")
        mm7 = _sem("p7_mm")
        ev7 = _sem("p7_ev")
        st7 = [_sem("p7_st0"), _sem("p7_st1")]
        mv7 = _sem("p7_mv")
        evv7 = _sem("p7_evv")
        sv7 = [_sem("p7_sv0"), _sem("p7_sv1")]
        blk = es.enter_context(nc.Block())

        @blk.sync
        def _(sync):
            sync.dma_start(out=prst[:, :].rearrange("p (h cc) -> p h cc", h=8),
                           in_=l1_proj_h.rearrange("h p cc -> p h cc")).then_inc(lds7, 16)
            for i in range(8):
                if i >= 2:
                    sync.wait_ge(cr7, i - 1)
                sync.dma_start(out=wvst[:, ts(i % 2, D)],
                               in_=l1_wqkvT_h[ts(i, 128), 2048:3072]).then_inc(ldr7[i % 2], 16)
            for oc in range(16):
                if oc >= 2:
                    sync.wait_ge(cq7, oc - 1)
                sync.dma_start(
                    out=wcst[:, ts(oc % 2, 1024)].rearrange("p (i cc) -> p i cc", i=8),
                    in_=l1_wqkvT_h[:, 128 * oc:128 * (oc + 1)]
                        .rearrange("(i p) cc -> p i cc", p=128)).then_inc(ldq7[oc % 2], 16)
            for b in range(4):
                if b >= 2:
                    sync.wait_ge(mm7, 16 * (b - 1))
                    sync.wait_ge(mv7, 4 * (b - 1))
                sync.dma_start(out=xblk[:, ts(b % 2, 4096)].rearrange("p (i cc) -> p i cc", i=8),
                               in_=xn1_sc.rearrange("p (i tt) -> p i tt", i=8)
                               [:, :, 512 * b:512 * (b + 1)]).then_inc(ldx7[b % 2], 16)

        @blk.vector
        def _(vector):
            vector.wait_ge(lds7, 16)
            vector.tensor_copy(proj16g[:, :], prst[:, :])
            for i in range(8):
                vector.wait_ge(ldr7[i % 2], 16 * (i // 2 + 1))
                vector.tensor_copy(wv16b[:, ts(i, D)], wvst[:, ts(i % 2, D)]).then_inc(cr7, 1)
            for oc in range(16):
                vector.wait_ge(ldq7[oc % 2], 16 * (oc // 2 + 1))
                vector.tensor_copy(wc16a[:, ts(oc, 1024)],
                                   wcst[:, ts(oc % 2, 1024)]).then_inc(cq7, 1)
            for b in range(4):
                for oc in range(16):
                    gi = b * 16 + oc
                    vector.wait_ge(mm7, gi + 1)
                    if gi >= 2:
                        vector.wait_ge(st7[gi % 2], 16 * (gi // 2))
                    vector.tensor_copy(qst[:, ts(gi % 2, 512)],
                                       ps7a[:, ts(gi % 2, 512)]).then_inc(ev7, 1)
                for t4 in range(4):
                    g = 4 * b + t4
                    vector.wait_ge(mv7, g + 1)
                    if g >= 2:
                        vector.wait_ge(sv7[g % 2], 64 * (g // 2))
                    vector.tensor_copy(vstg[:, ts(g % 2, 1024)],
                                       ps7v[:, ts(g % 2, 1024)]).then_inc(evv7, 1)

        @blk.tensor
        def _(tensor):
            for b in range(4):
                tensor.wait_ge(ldx7[b % 2], 16 * (b // 2 + 1))
                for oc in range(16):
                    gi = b * 16 + oc
                    tensor.wait_ge(cq7, oc + 1)
                    if gi >= 2:
                        tensor.wait_ge(ev7, gi - 1)
                    seg = ps7a[:, ts(gi % 2, 512)]
                    for ic in range(8):
                        mm = tensor.matmul(
                            seg, wc16a[:, 1024 * oc + 128 * ic:1024 * oc + 128 * (ic + 1)],
                            xblk[:, (b % 2) * 4096 + 512 * ic:(b % 2) * 4096 + 512 * (ic + 1)],
                            start=(ic == 0), stop=(ic == 7))
                    mm.then_inc(mm7, 1)
                if b == 0:
                    tensor.wait_ge(cr7, 8)
                for t4 in range(4):
                    g = 4 * b + t4
                    if g >= 2:
                        tensor.wait_ge(evv7, g - 1)
                    for half in range(2):
                        seg = ps7v[:, 1024 * (g % 2) + 512 * half:1024 * (g % 2) + 512 * (half + 1)]
                        for ic in range(8):
                            mm = tensor.matmul(
                                seg,
                                xblk[:, (b % 2) * 4096 + 512 * ic + 128 * t4:
                                     (b % 2) * 4096 + 512 * ic + 128 * t4 + 128],
                                wv16b[:, D * ic + 512 * half:D * ic + 512 * (half + 1)],
                                start=(ic == 0), stop=(ic == 7))
                    mm.then_inc(mv7, 1)

        @blk.gpsimd
        def _(gpsimd):
            for b in range(4):
                for oc in range(16):
                    gi = b * 16 + oc
                    gpsimd.wait_ge(ev7, gi + 1)
                    gpsimd.dma_start(out=qk1_sc[oc, :, 512 * b:512 * (b + 1)],
                                     in_=qst[:, ts(gi % 2, 512)]).then_inc(st7[gi % 2], 16)
                for t4 in range(4):
                    g = 4 * b + t4
                    gpsimd.wait_ge(evv7, g + 1)
                    for cb in range(4):
                        gpsimd.dma_start(
                            out=v1_sc2[4 * g + cb],
                            in_=vstg[32 * cb:32 * (cb + 1), ts(g % 2, 1024)],
                        ).then_inc(sv7[g % 2], 16)
            gpsimd.wait_ge(st7[0], 16 * 32)
            gpsimd.wait_ge(st7[1], 16 * 32)
            gpsimd.wait_ge(sv7[0], 64 * 8)
            gpsimd.wait_ge(sv7[1], 64 * 8)

    if upto <= 9:
        return c
    with ExitStack() as es:
        es.enter_context(nc.named_scope("p7b"))
        def _sb(n, s, d):
            return es.enter_context(nc.sbuf_tensor(n, s, d))

        def _ps(n, s, d=F32):
            return es.enter_context(nc.psum_tensor(n, s, d))

        _sem = mk_sem_scope(nc, es)

        vth = at("p7_vth", [32, 2 * 64 * 128], BF16, OFF_A)
        qfh = at("p7_qfh", [128, 2 * T], BF16, OFF_C)
        kfh = at("p7_kfh", [128, 2 * T], BF16, OFF_C + 8192)
        elt = at("p7_elt", [128, 2 * 1024], F32, OFF_C + 16384)
        zb7 = at("p7_zb", [128, T], F32, OFF_C + 24576)
        qp1z = _sb("p7_qp1z", [128, T], BF16)
        kp1h = _sb("p7_kp1h", [32, NT * 512], BF16)
        kvsb = _sb("p7_kvsb", [128, 64 * 128], BF16)
        kssb = _sb("p7_kssb", [128, 64], BF16)
        onec = _sb("p7_onec", [128, 1], BF16)
        zrow7 = _sb("p7_zrow", [1, T], F32)
        zcol7 = _sb("p7_zcol", [128, 16], F32)
        o1st = _sb("p7_o1st", [128, 2 * T], BF16)
        ps_a = _ps("p7b_psa", [128, 2 * 512])
        ps_c = _ps("p7b_psc", [128, 2 * 512])
        ps_v6 = _ps("p7b_psv", [128, 2048])
        ldh = [_sem("p7b_ldh0"), _sem("p7b_ldh1")]
        mq = _sem("p7b_mq")
        eu1 = _sem("p7b_eu1"); eu2 = _sem("p7b_eu2"); eu3 = _sem("p7b_eu3")
        mkv = _sem("p7b_mkv"); ekv = _sem("p7b_ekv"); eks = _sem("p7b_eks")
        mz = _sem("p7b_mz"); ez = _sem("p7b_ez")
        zch = _sem("p7b_zch"); vz7 = _sem("p7b_vz7"); qz = _sem("p7b_qz")
        mnum = _sem("p7b_mnum"); enum_ = _sem("p7b_enum")
        so1 = [_sem("p7b_so0"), _sem("p7b_so1")]
        blk = es.enter_context(nc.Block())

        QKU = 20

        @blk.sync
        def _(sync):
            for h in range(8):
                sl = h % 2
                if h >= 2:
                    if P7B_STOP >= 4:
                        sync.wait_ge(mnum, 2 * (h - 1))
                    elif P7B_STOP >= 2:
                        sync.wait_ge(ekv, (h - 1) * NT)
                    else:
                        sync.wait_ge(eu3, (h - 1) * QKU)
                sync.dma_start(out=qfh[:, ts(sl, T)], in_=qk1_sc[h]).then_inc(ldh[sl], 16)
                sync.dma_start(out=kfh[:, ts(sl, T)], in_=qk1_sc[8 + h]).then_inc(ldh[sl], 16)
                sync.dma_start(
                    out=vth[:, sl * 8192:(sl + 1) * 8192]
                        .rearrange("p (blk cc) -> p blk cc", blk=64),
                    in_=v1_sc2[:, :, 128 * h:128 * (h + 1)].rearrange("blk p cc -> p blk cc"),
                ).then_inc(ldh[sl], 16)

        @blk.vector
        def _(vector):
            vector.memset(onec[:, :], 1.0)
            for h in range(8):
                sl = h % 2
                vector.wait_ge(ldh[sl], 48 * (h // 2 + 1))
                for u in range(QKU):
                    gi = h * QKU + u
                    rows = 128 if u < 4 else 32
                    par = gi % 2
                    t0 = elt[0:rows, 1024 * par:1024 * par + 512]
                    t1 = elt[0:rows, 1024 * par + 512:1024 * par + 1024]
                    vector.wait_ge(mq, gi + 1)
                    if gi >= 2:
                        vector.wait_ge(eu3, gi - 1)
                    seg = (ps_a if u < 4 else ps_c)[0:rows, (gi % 2) * 512:(gi % 2) * 512 + 512]
                    vector.tensor_scalar(t0, seg, 0.0, None, ALU.min)
                    vector.tensor_scalar(t1, seg, 0.0, None, ALU.max).then_inc(eu1, 1)
                    if u < 4:
                        dst = qp1z[:, 512 * u:512 * (u + 1)]
                    else:
                        g = u - 4
                        dst = kp1h[0:32, 512 * g:512 * (g + 1)]
                    vector.wait_ge(eu2, gi + 1)
                    vector.tensor_tensor(dst, t0, t1, ALU.add).then_inc(eu3, 1)
                if P7B_STOP < 2:
                    continue
                for g in range(NT):
                    hg = h * NT + g
                    vector.wait_ge(mkv, hg + 1)
                    vector.tensor_copy(kvsb[:, 512 * g:512 * (g + 1)],
                                       ps_v6[:, 512 + 512 * (g % 2):1024 + 512 * (g % 2)],
                                       ).then_inc(ekv, 1)
                vector.tensor_copy(kssb[:, :], ps_v6[:, 0:64]).then_inc(eks, 1)
                if P7B_STOP < 3:
                    continue
                for rnd in range(4):
                    vector.wait_ge(mz, 4 * h + rnd + 1)
                    vector.tensor_copy(zrow7[0:1, 512 * rnd:512 * (rnd + 1)],
                                       ps_v6[0:1, 1536:2048]).then_inc(ez, 1)
                vector.drain()
                vector.tensor_scalar(zrow7[:, :], zrow7[:, :], 1e-8, None,
                                     ALU.add).then_inc(vz7, 1)
                vector.wait_ge(zch, 16 * (9 * h + 1))
                vector.reciprocal(zcol7[:, :], zcol7[:, :]).then_inc(vz7, 1)
                vector.wait_ge(zch, 16 * (9 * h + 9))
                vector.tensor_tensor(qp1z[:, :], qp1z[:, :], zb7[:, :],
                                     ALU.mult).then_inc(qz, 1)
                if P7B_STOP < 4:
                    continue
                for half in range(2):
                    nh = h * 2 + half
                    vector.wait_ge(mnum, nh + 1)
                    if h >= 2 and half == 0:
                        vector.wait_ge(so1[h % 2], 16 * (h // 2))
                    vector.tensor_copy(o1st[:, (h % 2) * T + 1024 * half:
                                            (h % 2) * T + 1024 * (half + 1)],
                                       ps_v6[:, 512:1536]).then_inc(enum_, 1)

        @blk.scalar
        def _(scalar):
            for h in range(8):
                for u in range(QKU):
                    gi = h * QKU + u
                    rows = 128 if u < 4 else 32
                    par = gi % 2
                    scalar.wait_ge(eu1, gi + 1)
                    scalar.activation(elt[0:rows, 1024 * par:1024 * par + 512],
                                      elt[0:rows, 1024 * par:1024 * par + 512],
                                      AF.Exp, bias=0.0, scale=1.0).then_inc(eu2, 1)

        @blk.tensor
        def _(tensor):
            for h in range(8):
                sl = h % 2
                tensor.wait_ge(ldh[sl], 48 * (h // 2 + 1))
                for u in range(4):
                    gi = h * QKU + u
                    if gi >= 2:
                        tensor.wait_ge(eu1, gi - 1)
                    tensor.matmul(ps_a[:, ts(gi % 2, 512)],
                                  proj16g[:, ts(h, 128)],
                                  qfh[:, sl * T + 512 * u:sl * T + 512 * (u + 1)],
                                  start=True, stop=True).then_inc(mq, 1)
                for g in range(NT):
                    gi = h * QKU + 4 + g
                    if gi >= 2:
                        tensor.wait_ge(eu1, gi - 1)
                    for cb in range(4):
                        mm = tensor.matmul(
                            ps_c[0:32, (gi % 2) * 512 + 128 * cb:(gi % 2) * 512 + 128 * (cb + 1)],
                            kfh[:, sl * T + 128 * g + 32 * cb:sl * T + 128 * g + 32 * (cb + 1)],
                            proj16g[:, ts(h, 128)],
                            start=True, stop=True)
                    mm.then_inc(mq, 1)
                if P7B_STOP < 2:
                    continue
                tensor.wait_ge(eu3, (h + 1) * QKU)
                if h >= 1:
                    if P7B_STOP >= 4:
                        tensor.wait_ge(enum_, 2 * h)
                    tensor.wait_ge(eks, h)
                for g in range(NT):
                    hg = h * NT + g
                    if hg >= 2:
                        tensor.wait_ge(ekv, hg - 1)
                    for cb in range(4):
                        kpb = kp1h[0:32, 512 * g + 128 * cb:512 * g + 128 * (cb + 1)]
                        tensor.matmul(
                            ps_v6[:, 512 + 512 * (g % 2) + 128 * cb:
                                  512 + 512 * (g % 2) + 128 * (cb + 1)],
                            kpb,
                            vth[0:32, sl * 8192 + (4 * g + cb) * 128:
                                sl * 8192 + (4 * g + cb) * 128 + 128],
                            start=True, stop=True)
                        mm = tensor.matmul(
                            ps_v6[:, 4 * g + cb:4 * g + cb + 1],
                            kpb, onec[0:32, 0:1],
                            start=True, stop=True)
                    mm.then_inc(mkv, 1)
                if P7B_STOP < 3:
                    continue
                tensor.wait_ge(eks, h + 1)
                for rnd in range(4):
                    if 4 * h + rnd >= 1:
                        tensor.wait_ge(ez, 4 * h + rnd)
                    for tt in range(4):
                        g = 4 * rnd + tt
                        for cb in range(4):
                            bi = 4 * g + cb
                            mm = tensor.matmul(
                                ps_v6[:, 1536 + 128 * tt + 32 * cb:
                                      1536 + 128 * tt + 32 * (cb + 1)][0:1, :],
                                kssb[:, bi:bi + 1],
                                qp1z[:, 32 * bi:32 * (bi + 1)],
                                start=True, stop=True)
                    mm.then_inc(mz, 1)
                tensor.wait_ge(qz, h + 1)
                if P7B_STOP < 4:
                    continue
                for half in range(2):
                    nh = h * 2 + half
                    if nh >= 1:
                        tensor.wait_ge(enum_, nh)
                    for bj in range(32):
                        bi = 32 * half + bj
                        mm = tensor.matmul(
                            ps_v6[:, 512 + 32 * bj:512 + 32 * bj + 32],
                            kvsb[:, 128 * bi:128 * (bi + 1)],
                            qp1z[:, 32 * bi:32 * (bi + 1)],
                            start=True, stop=True)
                    mm.then_inc(mnum, 1)

        @blk.gpsimd
        def _(gpsimd):
            for h in (range(8) if P7B_STOP >= 3 else []):
                gpsimd.wait_ge(vz7, 2 * h + 1)
                gpsimd.dma_start(out=zcol7[:, :],
                                 in_=zrow7[0:1, :].rearrange("a (pp j) -> a pp j", pp=128),
                                 ).then_inc(zch, 16)
                gpsimd.wait_ge(vz7, 2 * h + 2)
                gpsimd.dma_start(out=zb7[0:1, :].rearrange("a (pp j) -> a pp j", pp=128),
                                 in_=zcol7[:, :]).then_inc(zch, 16)
                k = 1
                for dd in range(7):
                    gpsimd.wait_ge(zch, 16 * (9 * h + 2 + dd))
                    gpsimd.dma_start(out=zb7[k:2 * k, :], in_=zb7[0:k, :]).then_inc(zch, 16)
                    k *= 2
                if P7B_STOP >= 4:
                    gpsimd.wait_ge(enum_, 2 * (h + 1))
                    gpsimd.dma_start(out=o1_sc[h],
                                     in_=o1st[:, ts(h % 2, T)]).then_inc(so1[h % 2], 16)
            if P7B_STOP >= 4:
                gpsimd.wait_ge(so1[0], 16 * 4)
                gpsimd.wait_ge(so1[1], 16 * 4)

    if upto <= 10:
        return c
    with ExitStack() as es:
        es.enter_context(nc.named_scope("p7c"))
        def _sb(n, s, d):
            return es.enter_context(nc.sbuf_tensor(n, s, d))

        def _ps(n, s, d=F32):
            return es.enter_context(nc.psum_tensor(n, s, d))

        _sem = mk_sem_scope(nc, es)

        wvst = _sb("p7c_wst", [128, 2 * D], F32)
        wout16 = _sb("p7c_wout16", [128, 8 * D], BF16)
        bout_s = _sb("p7c_bout", [1, D], F32)
        o1t = _sb("p7c_o1t", [128, 2 * 1024], BF16)
        ps7c = _ps("p7c_ps", [128, 2 * D])
        ldw = [_sem("p7c_ldw0"), _sem("p7c_ldw1")]
        ldo = [_sem("p7c_ldo0"), _sem("p7c_ldo1")]
        ldb = _sem("p7c_ldb")
        cst = _sem("p7c_cst")
        mmw = _sem("p7c_mmw")
        evw = _sem("p7c_evw")
        blk = es.enter_context(nc.Block())

        @blk.sync
        def _(sync):
            sync.dma_start(out=bout_s[:, :], in_=l1_bout_r).then_inc(ldb, 16)
            for i in range(8):
                if i >= 2:
                    sync.wait_ge(cst, i - 1)
                sync.dma_start(out=wvst[:, ts(i % 2, D)],
                               in_=l1_woutT_h[ts(i, 128), :]).then_inc(ldw[i % 2], 16)
            for g in range(NT):
                if g >= 2:
                    sync.wait_ge(mmw, g - 1)
                sync.dma_start(
                    out=o1t[:, ts(g % 2, 1024)].rearrange("p (fc cc) -> p fc cc", fc=8),
                    in_=o1_sc[:, :, 128 * g:128 * (g + 1)].rearrange("fc p cc -> p fc cc"),
                ).then_inc(ldo[g % 2], 16)

        @blk.vector
        def _(vector):
            for i in range(8):
                vector.wait_ge(ldw[i % 2], 16 * (i // 2 + 1))
                vector.tensor_copy(wout16[:, ts(i, D)], wvst[:, ts(i % 2, D)]).then_inc(cst, 1)
            for g in range(NT):
                vector.wait_ge(mmw, g + 1)
                vector.tensor_copy(s1_tm[:, ts(g, D)],
                                   ps7c[:, ts(g % 2, D)]).then_inc(evw, 1)

        @blk.tensor
        def _(tensor):
            tensor.wait_ge(ldb, 16)
            tensor.wait_ge(cst, 8)
            for g in range(NT):
                tensor.wait_ge(ldo[g % 2], 16 * (g // 2 + 1))
                if g >= 2:
                    tensor.wait_ge(evw, g - 1)
                for half in range(2):
                    seg = ps7c[:, (g % 2) * D + 512 * half:(g % 2) * D + 512 * (half + 1)]
                    tensor.matmul(seg, ones_r[0:1, 0:128], bout_s[0:1, ts(half, 512)],
                                  start=True, stop=False)
                    for fc in range(8):
                        mm = tensor.matmul(
                            seg, o1t[:, (g % 2) * 1024 + 128 * fc:(g % 2) * 1024 + 128 * (fc + 1)],
                            wout16[:, D * fc + 512 * half:D * fc + 512 * (half + 1)],
                            start=False, stop=(fc == 7))
                mm.then_inc(mmw, 1)

    if upto <= 11:
        return c
    with ExitStack() as es:
        es.enter_context(nc.named_scope("p8"))
        def _sb(n, s, d):
            return es.enter_context(nc.sbuf_tensor(n, s, d))

        _sem = mk_sem_scope(nc, es)

        x1b = _sb("p8_x1b", [128, 2 * D], F32)
        sg8 = _sb("p8_sg", [128, 2 * D], F32)
        ot8 = _sb("p8_ot", [128, 2 * D], F32)
        ldx8 = [_sem("p8_ldx0"), _sem("p8_ldx1")]
        v8 = _sem("p8_v8")
        st8 = [_sem("p8_st0"), _sem("p8_st1")]
        blk = es.enter_context(nc.Block())

        @blk.sync
        def _(sync):
            for g in range(NT):
                if g >= 2:
                    sync.wait_ge(v8, 2 * (g - 1))
                sync.dma_start(out=x1b[:, ts(g % 2, D)],
                               in_=x1_sc[ts(g, 128), :]).then_inc(ldx8[g % 2], 16)

        @blk.vector
        def _(vector):
            for g in range(NT):
                par = g % 2
                vector.wait_ge(ldx8[par], 16 * (g // 2 + 1))
                if g >= 2:
                    vector.wait_ge(st8[par], 16 * (g // 2))
                vector.tensor_tensor(sg8[:, ts(par, D)], s1_tm[:, ts(g, D)],
                                     g1_tm[:, ts(g, D)], ALU.mult).then_inc(v8, 1)
                vector.drain()
                vector.tensor_tensor(ot8[:, ts(par, D)], sg8[:, ts(par, D)],
                                     x1b[:, ts(par, D)], ALU.add).then_inc(v8, 1)

        @blk.gpsimd
        def _(gpsimd):
            for g in range(NT):
                gpsimd.wait_ge(v8, 2 * g + 2)
                gpsimd.dma_start(out=out[ts(g, 128), :],
                                 in_=ot8[:, ts(g % 2, D)]).then_inc(st8[g % 2], 16)
            gpsimd.wait_ge(st8[0], 16 * (NT // 2))
            gpsimd.wait_ge(st8[1], 16 * (NT // 2))

    if upto <= 12:
        return c
    tap_ksc = c.tap("t_ksc", [HEADS, 64, S], BF16)
    tap_vsc = c.tap("t_vsc", [HEADS, S, 65], BF16)
    tap_xnfm = c.tap("t_xnfm", [128, 8 * T], BF16)
    tap_qp = c.tap("t_qp", [8, 128, T], BF16)
    tap_g = c.tap("t_g", [128, NT * D], BF16)
    tap_o = c.tap("t_o", [8, 128, T], BF16)
    tap_s = c.tap("t_s", [128, NT * D], BF16)
    tap_x1 = c.tap("t_x1", [T, D], F32)
    tap_xn1 = c.tap("t_xn1", [128, 8 * T], BF16)
    tap_qk1 = c.tap("t_qk1", [16, 128, T], BF16)
    tap_v1 = c.tap("t_v1", [NT * 4, 32, 1024], BF16)
    tap_o1 = c.tap("t_o1", [8, 128, T], BF16)
    tap_s1 = c.tap("t_s1", [128, NT * D], BF16)
    tap_g1 = c.tap("t_g1", [128, NT * D], BF16)
    with (
        nc.sbuf_tensor("zero_t", [128, D], F32) as zero_t,
        nc.semaphore("dbg_sem") as dsem,
        nc.semaphore("dbg_semv") as dsemv,
        nc.Block() as blk,
    ):
        @blk.vector
        def _(vector):
            vector.memset(zero_t[:, :], 0.0).then_inc(dsemv, 1)

        @blk.gpsimd
        def _(gpsimd):
            gpsimd.wait_ge(dsemv, 1)
            n = 0
            if tap_ksc is not None:
                gpsimd.dma_start(out=tap_ksc, in_=k_sc).then_inc(dsem, 16); n += 1
            if tap_vsc is not None:
                gpsimd.dma_start(out=tap_vsc, in_=v_sc).then_inc(dsem, 16); n += 1
            if tap_xnfm is not None:
                gpsimd.dma_start(out=tap_xnfm, in_=xn_fm[:, :]).then_inc(dsem, 16); n += 1
            if tap_qp is not None:
                for i in range(8):
                    gpsimd.dma_start(out=tap_qp[i], in_=q_pair[i][:, :]).then_inc(dsem, 16)
                n += 8
            if tap_g is not None:
                gpsimd.dma_start(out=tap_g, in_=g_tm[:, :]).then_inc(dsem, 16)
                n += 1
            if tap_o is not None:
                for i in range(8):
                    gpsimd.dma_start(out=tap_o[i], in_=o_pair[i][:, :]).then_inc(dsem, 16)
                n += 8
            if tap_s is not None:
                gpsimd.dma_start(out=tap_s, in_=h1_tm[:, :]).then_inc(dsem, 16)
                n += 1
            if tap_x1 is not None:
                gpsimd.dma_start(out=tap_x1, in_=x1_sc).then_inc(dsem, 16)
                n += 1
            if tap_xn1 is not None:
                gpsimd.dma_start(out=tap_xn1, in_=xn1_sc).then_inc(dsem, 16)
                n += 1
            if tap_qk1 is not None:
                gpsimd.dma_start(out=tap_qk1, in_=qk1_sc).then_inc(dsem, 16); n += 1
            if tap_v1 is not None:
                gpsimd.dma_start(out=tap_v1, in_=v1_sc2).then_inc(dsem, 16); n += 1
            if tap_o1 is not None:
                gpsimd.dma_start(out=tap_o1, in_=o1_sc).then_inc(dsem, 16); n += 1
            if tap_s1 is not None:
                gpsimd.dma_start(out=tap_s1, in_=s1_tm[:, :]).then_inc(dsem, 16); n += 1
            if tap_g1 is not None:
                gpsimd.dma_start(out=tap_g1, in_=g1_tm[:, :]).then_inc(dsem, 16); n += 1
            gpsimd.wait_ge(dsem, 16 * n)

    return c



def kernel(**inputs):
    in_maps = make_in_maps(inputs)
    c = build_program()
    res = run_bass_kernel_spmd(c.nc, in_maps, core_ids=list(range(8)))
    out = np.empty((B, S, D), np.float32)
    for cc in range(8):
        b, r = divmod(cc, 4)
        out[b, r * T:(r + 1) * T] = res.results[cc]["out_own"]
    return out
